# version 18
# baseline (speedup 1.0000x reference)
import numpy as np
import ml_dtypes
from contextlib import ExitStack

import concourse.bass as bass
import concourse.mybir as mybir
from concourse.bass_utils import run_bass_kernel_spmd

F32 = mybir.dt.float32
BF16 = mybir.dt.bfloat16
AF = mybir.ActivationFunctionType
ALU = mybir.AluOpType
AX = mybir.AxisListType

D = 1024
NS = 8
HD = 64
NH = 16
TPC = 2048
WIN = 1024
EPS = 1e-5
ALPHA = 1.702
LIMIT = 7.0
BOFF = 20.0
SEQ = 8192
E = 32
F = 1024

CFG = dict(n_tt=16, n_wt=8, n_pairs=8, n_e=32)


class Sem:
    def __init__(self, es, nc, name):
        self.h = es.enter_context(nc.semaphore(name))
        self.n = 0

    def inc(self, instr, k=1):
        instr.then_inc(self.h, k)
        self.n += k
        return self.n


def build_program(cfg=None):
    cfg = dict(CFG, **(cfg or {}))
    n_tt, n_wt, n_pairs, n_e = cfg["n_tt"], cfg["n_wt"], cfg["n_pairs"], cfg["n_e"]
    n_tok = n_tt * 128
    n_win = n_wt * 128
    n_qc = n_tok // 512
    n_half = 2
    n_tc = n_tok // 512 // n_half
    assert n_tc >= 1 and n_qc >= 2

    nc = bass.Bass("TRN2", target_bir_lowering=False)

    xo = nc.dram_tensor("xo", [n_tok, D], F32, kind="ExternalInput").ap()
    xw = nc.dram_tensor("xw", [n_win, D], F32, kind="ExternalInput").ap()
    wq = nc.dram_tensor("wq", [D, D], F32, kind="ExternalInput").ap()
    wk = nc.dram_tensor("wk", [D, D], F32, kind="ExternalInput").ap()
    wv = nc.dram_tensor("wv", [D, D], F32, kind="ExternalInput").ap()
    wo = nc.dram_tensor("wo", [D, D], F32, kind="ExternalInput").ap()
    bq_r = nc.dram_tensor("bq_r", [128, NS], F32, kind="ExternalInput").ap()
    bk_r = nc.dram_tensor("bk_r", [128, NS], F32, kind="ExternalInput").ap()
    bv_r = nc.dram_tensor("bv_r", [128, NS], F32, kind="ExternalInput").ap()
    bo_row = nc.dram_tensor("bo_row", [1, D], F32, kind="ExternalInput").ap()
    nsc_row = nc.dram_tensor("nsc_row", [1, D], F32, kind="ExternalInput").ap()
    gwT_r = nc.dram_tensor("gwT_r", [128, NS * E], F32, kind="ExternalInput").ap()
    gb_row = nc.dram_tensor("gb_row", [1, E], F32, kind="ExternalInput").ap()
    w1h = nc.dram_tensor("w1h", [n_e, 16, 128, NS * 128], F32, kind="ExternalInput").ap()
    b1h = nc.dram_tensor("b1h", [128, n_e * 16], F32, kind="ExternalInput").ap()
    w2h = nc.dram_tensor("w2h", [n_e, 128, NS * F], F32, kind="ExternalInput").ap()
    b2d = nc.dram_tensor("b2d", [E, F], F32, kind="ExternalInput").ap()
    alibi = nc.dram_tensor("alibi", [128, NH * 8], F32, kind="ExternalInput").ap()
    eye_bf = nc.dram_tensor("eye_bf", [128, 128], BF16, kind="ExternalInput").ap()
    eye_f32 = nc.dram_tensor("eye_f32", [128, 128], F32, kind="ExternalInput").ap()
    s_scr = nc.dram_tensor("s_scr", [n_tok, D], BF16, kind="Internal").ap()
    yout = nc.dram_tensor("yout", [n_tok, D], F32, kind="ExternalOutput").ap()

    es = ExitStack()
    with es:
        tfT = es.enter_context(nc.sbuf_tensor("tfT", [128, NS * n_tok], BF16))
        combT = es.enter_context(nc.sbuf_tensor("combT", [E, n_tok], BF16))
        comb_all = es.enter_context(nc.sbuf_tensor("comb_all", [128, CFG['n_tt'] * E], F32))
        g_all = es.enter_context(nc.sbuf_tensor("g_all", [128, n_tt * E], F32))
        mall = es.enter_context(nc.sbuf_tensor("mall", [128, n_tt], F32))
        rstdall = es.enter_context(nc.sbuf_tensor("rstdall", [128, n_tt], F32))
        bq_sb = es.enter_context(nc.sbuf_tensor("bq_sb", [128, NS], F32))
        bk_sb = es.enter_context(nc.sbuf_tensor("bk_sb", [128, NS], F32))
        bv_sb = es.enter_context(nc.sbuf_tensor("bv_sb", [128, NS], F32))
        gwT_sb = es.enter_context(nc.sbuf_tensor("gwT_sb", [128, NS * E], F32))
        alibi_sb = es.enter_context(nc.sbuf_tensor("alibi_sb", [128, NH * 8], F32))
        eyebf_sb = es.enter_context(nc.sbuf_tensor("eyebf_sb", [128, 128], BF16))
        eyef_sb = es.enter_context(nc.sbuf_tensor("eyef_sb", [128, 128], F32))
        ones_f = es.enter_context(nc.sbuf_tensor("ones_f", [1, 128], F32))
        cbones = es.enter_context(nc.sbuf_tensor("cbones", [1, 128], BF16))
        nsc_bc = es.enter_context(nc.sbuf_tensor("nsc_bc", [128, D], F32))
        gb_bc = es.enter_context(nc.sbuf_tensor("gb_bc", [128, E], F32))
        bo_bc = es.enter_context(nc.sbuf_tensor("bo_bc", [128, D], F32))
        b1_sb = es.enter_context(nc.sbuf_tensor("b1_sb", [128, n_e * 16], F32))
        b2_sb = es.enter_context(nc.sbuf_tensor("b2_sb", [E, F], BF16))
        eps_sb = es.enter_context(nc.sbuf_tensor("eps_sb", [128, 1], F32))
        eps2_sb = es.enter_context(nc.sbuf_tensor("eps2_sb", [128, 1], F32))
        es0 = ExitStack()
        bo_sbr = es0.enter_context(nc.sbuf_tensor("bo_sbr", [1, D], F32))
        nsc_sbr = es0.enter_context(nc.sbuf_tensor("nsc_sbr", [1, D], F32))
        gb_sbr = es0.enter_context(nc.sbuf_tensor("gb_sbr", [1, E], F32))

        s_x = [Sem(es, nc, f"s_x{i}") for i in range(2)]
        s_sl = [Sem(es, nc, f"s_sl{i}") for i in range(2)]
        s_ss = [Sem(es, nc, f"s_ss{i}") for i in range(2)]
        s_y = [Sem(es, nc, f"s_y{i}") for i in range(2)]
        s_c1 = Sem(es, nc, "s_c1")
        s_c2 = Sem(es, nc, "s_c2")
        s_wat = [Sem(es, nc, f"s_wat{i}") for i in range(2)]
        s_w1 = [Sem(es, nc, f"s_w1{i}") for i in range(6)]
        s_w2 = Sem(es, nc, "s_w2")
        s_x6 = [Sem(es, nc, f"s_x6{i}") for i in range(4)]
        s_sl6 = [Sem(es, nc, f"s_sl6{i}") for i in range(4)]
        P_pe = Sem(es, nc, "P_pe")
        P_dve = Sem(es, nc, "P_dve")
        P_act = Sem(es, nc, "P_act")
        s_lc = Sem(es, nc, "s_lc")

        with nc.Block() as blk:
            @blk.sync
            def _(sy):
                for dst, src in ((bq_sb, bq_r), (bk_sb, bk_r), (bv_sb, bv_r),
                                 (bo_sbr, bo_row), (nsc_sbr, nsc_row), (gb_sbr, gb_row),
                                 (alibi_sb, alibi), (eyebf_sb, eye_bf), (eyef_sb, eye_f32),
                                 (gwT_sb, gwT_r), (b1_sb, b1h)):
                    sy.dma_start(out=dst[:], in_=src[:, :]).then_inc(s_c1.h, 16)
                    s_c1.n += 16

            @blk.gpsimd
            def _(gp):
                gp.memset(ones_f[:], 1.0)
                gp.memset(cbones[:], 1.0 / ALPHA)
                gp.memset(eps_sb[:], EPS)
                gp.memset(eps2_sb[:], EPS * EPS)
                gp.dma_start(out=b2_sb[:], in_=b2d[:, :]).then_inc(s_c2.h, 16)
                s_c2.n += 16

            @blk.vector
            def _(v):
                v.wait_ge(s_c1.h, s_c1.n)
                v.wait_ge(s_c2.h, s_c2.n)

        with nc.psum_tensor("pbc2", [128, 1024], F32) as pbc2, \
             nc.psum_tensor("pbc3", [128, 1024], F32) as pbc3, \
             nc.psum_tensor("pbc4", [128, E], F32) as pbc4:
            with nc.Block() as blk:
                @blk.tensor
                def _(te):
                    te.matmul(pbc2[:, 0:512], ones_f[:, :], bo_sbr[:, 0:512], start=True, stop=True)
                    te.matmul(pbc2[:, 512:1024], ones_f[:, :], bo_sbr[:, 512:1024], start=True, stop=True)
                    te.matmul(pbc3[:, 0:512], ones_f[:, :], nsc_sbr[:, 0:512], start=True, stop=True)
                    te.matmul(pbc3[:, 512:1024], ones_f[:, :], nsc_sbr[:, 512:1024], start=True, stop=True)
                    m = te.matmul(pbc4[:, :], ones_f[:, :], gb_sbr[:, :], start=True, stop=True)
                    P_pe.inc(m, 1)

                @blk.vector
                def _(v):
                    v.wait_ge(P_pe.h, P_pe.n)
                    v.tensor_copy(bo_bc[:], pbc2[:, :])
                    v.tensor_copy(nsc_bc[:], pbc3[:, :])
                    c = v.tensor_copy(gb_bc[:], pbc4[:, :])
                    P_dve.inc(c, 1)

                @blk.scalar
                def _(s):
                    s.wait_ge(P_dve.h, P_dve.n)

        es0.close()

        with ExitStack() as sc1:
            xnT_own = sc1.enter_context(nc.sbuf_tensor("xnT_own", [128, NS * n_tok], BF16))
            xnT_win = sc1.enter_context(nc.sbuf_tensor("xnT_win", [128, NS * n_win], BF16))
            sc1a = ExitStack()
            xb = [sc1a.enter_context(nc.sbuf_tensor(f"xb{i}", [128, D], F32)) for i in range(4)]
            st6 = [sc1a.enter_context(nc.sbuf_tensor(f"st6{i}", [128, 12], F32)) for i in range(4)]
            mv = [sc1a.enter_context(nc.sbuf_tensor(f"mv{i}", [128, 2], F32)) for i in range(4)]
            sdb = [sc1a.enter_context(nc.sbuf_tensor(f"sdb{i}", [128, 2], F32)) for i in range(4)]
            rstdb = [sc1a.enter_context(nc.sbuf_tensor(f"rstdb{i}", [128, 2], F32)) for i in range(4)]
            xnbf = [sc1a.enter_context(nc.sbuf_tensor(f"xnbf{i}", [128, D], BF16)) for i in range(4)]
            tf32 = [sc1a.enter_context(nc.sbuf_tensor(f"tf32{i}", [128, D], F32)) for i in range(4)]
            tfTt = [sc1a.enter_context(nc.sbuf_tensor(f"tfTt{i}", [128, D], F32)) for i in range(4)]
            gTsb = sc1a.enter_context(nc.sbuf_tensor("gTsb", [E, 128], F32))

            with nc.psum_tensor("trbf0", [128, 1024], BF16) as trbfA, \
                 nc.psum_tensor("trbf1", [128, 1024], BF16) as trbfB, \
                 nc.psum_tensor("tr320", [128, 1024], F32) as tr32A, \
                 nc.psum_tensor("tr321", [128, 1024], F32) as tr32B, \
                 nc.psum_tensor("gTp", [E, 128], F32) as gTp, \
                 nc.psum_tensor("g2p", [128, E], F32) as g2p:
                trbf = [trbfA, trbfB]
                tr32 = [tr32A, tr32B]

                DVE0, ACT0, PE0 = P_dve.n, P_act.n, P_pe.n
                SX0 = [s_x[0].n, s_x[1].n]

                def dveA(t): return DVE0 + 7 * t + 1
                def dveC(t): return DVE0 + 7 * t + 2
                def dveD(t): return DVE0 + 7 * t + 3
                def dveE(t): return DVE0 + 7 * t + 4
                def dveF(t): return DVE0 + 7 * t + 5
                def dveGm(t): return DVE0 + 7 * t + 6
                def dveG(t): return DVE0 + 7 * t + 7
                def actSD(t): return ACT0 + 2 * t + 2
                def peTB(t): return PE0 + 4 * t + 1
                def peT32(t): return PE0 + 4 * t + 2
                def peGm(t): return PE0 + 4 * t + 3
                def peG(t): return PE0 + 4 * t + 4

                with nc.Block() as blk:
                    @blk.sync
                    def _(sy):
                        for t in range(n_tt):
                            pi = t % 4
                            if t >= 4:
                                sy.wait_ge(P_dve.h, dveD(t - 4))
                            sy.dma_start(out=xb[pi][:], in_=xo[t * 128:(t + 1) * 128, :]) \
                              .then_inc(s_x6[pi].h, 16)
                            s_x6[pi].n += 16

                    @blk.vector
                    def _(v):
                        SX60 = [s_x6[i].n - 16 * ((n_tt + 3 - i) // 4) for i in range(4)]
                        for t in range(n_tt):
                            pi = t % 4
                            v.wait_ge(s_x6[pi].h, SX60[pi] + 16 * (t // 4 + 1))
                            if t >= 4:
                                v.wait_ge(P_act.h, actSD(t - 4))
                            v.bn_stats(st6[pi][:, 0:6], xb[pi][:, 0:512])
                            v.bn_stats(st6[pi][:, 6:12], xb[pi][:, 512:1024])
                            v.drain()
                            i = v.bn_aggr(mv[pi][:], st6[pi][:])
                            P_dve.inc(i, 1)
                            v.wait_ge(P_act.h, actSD(t))
                            v.reciprocal(rstdb[pi][:], sdb[pi][:])
                            v.drain()
                            v.tensor_copy(mall[:, t:t + 1], mv[pi][:, 0:1])
                            v.tensor_copy(rstdall[:, t:t + 1], rstdb[pi][:, 0:1])
                            i = v.tensor_scalar(out=xnbf[pi][:], in0=xb[pi][:], scalar1=mv[pi][:, 0:1],
                                                scalar2=rstdb[pi][:, 0:1], op0=ALU.subtract, op1=ALU.mult)
                            P_dve.inc(i, 1)
                            v.drain()
                            v.tensor_scalar(out=tf32[pi][:], in0=xb[pi][:], scalar1=mv[pi][:, 0:1],
                                            scalar2=rstdb[pi][:, 1:2], op0=ALU.subtract, op1=ALU.mult)
                            v.drain()
                            i = v.tensor_tensor(out=tf32[pi][:], in0=tf32[pi][:], in1=nsc_bc[:], op=ALU.mult)
                            P_dve.inc(i, 1)
                            v.wait_ge(P_pe.h, peTB(t))
                            dst = bass.AP(xnT_own, t * 128, [[NS * n_tok, 128], [n_tok, NS], [1, 128]])
                            src = bass.AP(trbf[t % 2], 0, [[1024, 128], [128, NS], [1, 128]])
                            i = v.tensor_copy(dst, src)
                            P_dve.inc(i, 1)
                            v.wait_ge(P_pe.h, peT32(t))
                            v.tensor_copy(tfTt[pi][:], tr32[t % 2][:, :])
                            dst2 = bass.AP(tfT, t * 128, [[NS * n_tok, 128], [n_tok, NS], [1, 128]])
                            i = v.tensor_copy(dst2, tr32[t % 2][:, :])
                            P_dve.inc(i, 1)
                            v.wait_ge(P_pe.h, peGm(t))
                            i = v.tensor_copy(gTsb[:, :], gTp[:, :])
                            P_dve.inc(i, 1)
                            v.wait_ge(P_pe.h, peG(t))
                            i = v.tensor_tensor(out=g_all[:, t * E:(t + 1) * E], in0=g2p[:, :],
                                                in1=gb_bc[:], op=ALU.add)
                            P_dve.inc(i, 1)
                        assert P_dve.n == DVE0 + 7 * n_tt

                    @blk.scalar
                    def _(s):
                        for t in range(n_tt):
                            pi = t % 4
                            s.wait_ge(P_dve.h, dveA(t))
                            P_act.inc(s.activation(out=sdb[pi][:, 0:1], in_=mv[pi][:, 1:2],
                                                   func=AF.Sqrt, bias=eps_sb[:]), 1)
                            P_act.inc(s.activation(out=sdb[pi][:, 1:2], in_=mv[pi][:, 1:2],
                                                   func=AF.Sqrt, scale=1.0 + EPS,
                                                   bias=eps2_sb[:]), 1)
                        assert P_act.n == ACT0 + 2 * n_tt

                    @blk.tensor
                    def _(te):
                        for t in range(n_tt):
                            pi = t % 4
                            te.wait_ge(P_dve.h, dveC(t))
                            if t >= 2:
                                te.wait_ge(P_dve.h, dveE(t - 2))
                            for hs in range(NS):
                                m = te.transpose(trbf[t % 2][:, hs * 128:(hs + 1) * 128],
                                                 xnbf[pi][:, hs * 128:(hs + 1) * 128], eyebf_sb[:])
                            P_pe.inc(m, 1)
                            te.wait_ge(P_dve.h, dveD(t))
                            if t >= 2:
                                te.wait_ge(P_dve.h, dveF(t - 2))
                            for hs in range(NS):
                                m = te.transpose(tr32[t % 2][:, hs * 128:(hs + 1) * 128],
                                                 tf32[pi][:, hs * 128:(hs + 1) * 128], eyef_sb[:])
                            P_pe.inc(m, 1)
                            te.wait_ge(P_dve.h, dveF(t))
                            if t >= 1:
                                te.wait_ge(P_dve.h, dveGm(t - 1))
                            for hs in range(NS):
                                m = te.matmul(gTp[:, :], gwT_sb[:, hs * E:(hs + 1) * E],
                                              tfTt[pi][:, hs * 128:(hs + 1) * 128],
                                              start=(hs == 0), stop=(hs == NS - 1))
                            P_pe.inc(m, 1)
                            te.wait_ge(P_dve.h, dveGm(t))
                            if t >= 1:
                                te.wait_ge(P_dve.h, dveG(t - 1))
                            m = te.transpose(g2p[:, :], gTsb[:, :], eyef_sb[0:32, 0:32])
                            P_pe.inc(m, 1)
                        assert P_pe.n == PE0 + 4 * n_tt

                DVE1, ACT1, PE1 = P_dve.n, P_act.n, P_pe.n
                SX1 = [s_x[0].n, s_x[1].n]

                def dveWA(t): return DVE1 + 3 * t + 1
                def dveWC(t): return DVE1 + 3 * t + 2
                def dveWE(t): return DVE1 + 3 * t + 3
                def actWSD(t): return ACT1 + t + 1
                def peWTB(t): return PE1 + t + 1

                with nc.Block() as blk:
                    @blk.sync
                    def _(sy):
                        for t in range(n_wt):
                            pi = t % 4
                            if t >= 4:
                                sy.wait_ge(P_dve.h, dveWC(t - 4))
                            sy.dma_start(out=xb[pi][:], in_=xw[t * 128:(t + 1) * 128, :]) \
                              .then_inc(s_x6[pi].h, 16)
                            s_x6[pi].n += 16

                    @blk.vector
                    def _(v):
                        SX61 = [s_x6[i].n - 16 * ((n_wt + 3 - i) // 4) for i in range(4)]
                        for t in range(n_wt):
                            pi = t % 4
                            v.wait_ge(s_x6[pi].h, SX61[pi] + 16 * (t // 4 + 1))
                            if t >= 4:
                                v.wait_ge(P_act.h, actWSD(t - 4))
                            v.bn_stats(st6[pi][:, 0:6], xb[pi][:, 0:512])
                            v.bn_stats(st6[pi][:, 6:12], xb[pi][:, 512:1024])
                            v.drain()
                            i = v.bn_aggr(mv[pi][:], st6[pi][:])
                            P_dve.inc(i, 1)
                            v.wait_ge(P_act.h, actWSD(t))
                            v.reciprocal(rstdb[pi][:, 0:1], sdb[pi][:, 0:1])
                            v.drain()
                            i = v.tensor_scalar(out=xnbf[pi][:], in0=xb[pi][:], scalar1=mv[pi][:, 0:1],
                                                scalar2=rstdb[pi][:, 0:1], op0=ALU.subtract, op1=ALU.mult)
                            P_dve.inc(i, 1)
                            v.wait_ge(P_pe.h, peWTB(t))
                            dst = bass.AP(xnT_win, t * 128, [[NS * n_win, 128], [n_win, NS], [1, 128]])
                            src = bass.AP(trbf[t % 2], 0, [[1024, 128], [128, NS], [1, 128]])
                            i = v.tensor_copy(dst, src)
                            P_dve.inc(i, 1)

                    @blk.scalar
                    def _(s):
                        for t in range(n_wt):
                            pi = t % 4
                            s.wait_ge(P_dve.h, dveWA(t))
                            P_act.inc(s.activation(out=sdb[pi][:, 0:1], in_=mv[pi][:, 1:2],
                                                   func=AF.Sqrt, bias=eps_sb[:]), 1)

                    @blk.tensor
                    def _(te):
                        for t in range(n_wt):
                            pi = t % 4
                            te.wait_ge(P_dve.h, dveWC(t))
                            if t >= 2:
                                te.wait_ge(P_dve.h, dveWE(t - 2))
                            for hs in range(NS):
                                m = te.transpose(trbf[t % 2][:, hs * 128:(hs + 1) * 128],
                                                 xnbf[pi][:, hs * 128:(hs + 1) * 128], eyebf_sb[:])
                            P_pe.inc(m, 1)

            m8b = [sc1a.enter_context(nc.sbuf_tensor(f"m8b{i}", [128, 8], F32)) for i in range(2)]
            nmx = [sc1a.enter_context(nc.sbuf_tensor(f"nmx{i}", [128, 1], F32)) for i in range(2)]
            maskb = [sc1a.enter_context(nc.sbuf_tensor(f"maskb{i}", [128, E], F32)) for i in range(2)]
            expb = [sc1a.enter_context(nc.sbuf_tensor(f"expb{i}", [128, E], F32)) for i in range(2)]
            denb = [sc1a.enter_context(nc.sbuf_tensor(f"denb{i}", [128, 1], F32)) for i in range(2)]
            combb = [sc1a.enter_context(nc.sbuf_tensor(f"combb{i}", [128, E], F32)) for i in range(2)]

            DVE2, ACT2, PE2 = P_dve.n, P_act.n, P_pe.n

            def dveH(t): return DVE2 + 3 * t + 1
            def dveJ(t): return DVE2 + 3 * t + 2
            def dveL(t): return DVE2 + 3 * t + 3
            def actI(t): return ACT2 + t + 1
            def peK(t): return PE2 + t + 1

            with nc.psum_tensor("ptr0", [E, 128], F32) as ptr0, \
                 nc.psum_tensor("ptr1", [E, 128], F32) as ptr1:
                ptrs = [ptr0, ptr1]
                with nc.Block() as blk:
                    @blk.vector
                    def _(v):
                        for t in range(n_tt):
                            pi = t % 2
                            gsl = g_all[:, t * E:(t + 1) * E]
                            if t >= 2:
                                v.wait_ge(P_act.h, actI(t - 2))
                            v.max(m8b[pi][:], gsl)
                            v.drain()
                            v.tensor_scalar(out=nmx[pi][:], in0=m8b[pi][:, 0:1], scalar1=-1.0,
                                            scalar2=None, op0=ALU.mult)
                            i = v.tensor_scalar(out=maskb[pi][:], in0=gsl, scalar1=m8b[pi][:, 3:4],
                                                scalar2=None, op0=ALU.is_ge)
                            P_dve.inc(i, 1)
                            v.wait_ge(P_act.h, actI(t))
                            v.tensor_tensor(out=expb[pi][:], in0=expb[pi][:], in1=maskb[pi][:],
                                            op=ALU.mult)
                            v.drain()
                            v.reduce_sum(denb[pi][:], expb[pi][:], axis=AX.X)
                            v.drain()
                            v.reciprocal(denb[pi][:], denb[pi][:])
                            v.drain()
                            i = v.tensor_scalar(out=combb[pi][:], in0=expb[pi][:], scalar1=denb[pi][:],
                                                scalar2=None, op0=ALU.mult)
                            P_dve.inc(i, 1)
                            v.drain()
                            v.tensor_scalar(out=comb_all[:, t * E:(t + 1) * E], in0=combb[pi][:],
                                            scalar1=1.0 / ALPHA, scalar2=None, op0=ALU.mult)
                            v.wait_ge(P_pe.h, peK(t))
                            i = v.tensor_copy(combT[:, t * 128:(t + 1) * 128], ptrs[pi][:, :])
                            P_dve.inc(i, 1)

                    @blk.scalar
                    def _(s):
                        for t in range(n_tt):
                            pi = t % 2
                            s.wait_ge(P_dve.h, dveH(t))
                            P_act.inc(s.activation(out=expb[pi][:], in_=g_all[:, t * E:(t + 1) * E],
                                                   func=AF.Exp, bias=nmx[pi][:]), 1)

                    @blk.tensor
                    def _(te):
                        for t in range(n_tt):
                            pi = t % 2
                            te.wait_ge(P_dve.h, dveJ(t))
                            if t >= 2:
                                te.wait_ge(P_dve.h, dveL(t - 2))
                            P_pe.inc(te.transpose(ptrs[pi][:, :], combb[pi][:, :], eyef_sb[:]), 1)

            sc1a.close()

            wo_sb = sc1.enter_context(nc.sbuf_tensor("wo_sb", [128, NS * D], BF16))
            attnT = sc1.enter_context(nc.sbuf_tensor("attnT", [128, n_pairs * n_tok], BF16))
            wpb = [sc1.enter_context(nc.sbuf_tensor(f"wpb{i}", [128, 3 * 1024], BF16)) for i in range(2)]
            kTb = [sc1.enter_context(nc.sbuf_tensor(f"kTb{i}", [128, n_win], BF16)) for i in range(2)]
            vtb = [sc1.enter_context(nc.sbuf_tensor(f"vtb{i}", [128, n_win], BF16)) for i in range(2)]
            vsb = [sc1.enter_context(nc.sbuf_tensor(f"vsb{i}", [128, n_wt * 2 * 65], BF16)) for i in range(2)]
            qTb = [sc1.enter_context(nc.sbuf_tensor(f"qTb{i}", [128, n_tok], BF16)) for i in range(2)]
            pbuf = [sc1.enter_context(nc.sbuf_tensor(f"pbuf{i}", [128, 512], BF16)) for i in range(4)]
            rlrow = [sc1.enter_context(nc.sbuf_tensor(f"rlrow{i}", [1, 512], F32)) for i in range(2)]
            rlbc = [sc1.enter_context(nc.sbuf_tensor(f"rlbc{i}", [64, 512], F32)) for i in range(2)]
            sstage = [sc1.enter_context(nc.sbuf_tensor(f"sstage{i}", [128, D], BF16)) for i in range(2)]

            with nc.psum_tensor("kvps", [128, 1024], F32) as kvps, \
                 nc.psum_tensor("trb2", [128, 1024], BF16) as trb2, \
                 nc.psum_tensor("scpP", [128, 1024], F32) as scpP, \
                 nc.psum_tensor("avpA", [128, 512], F32) as avpA, \
                 nc.psum_tensor("avpB", [128, 512], F32) as avpB, \
                 nc.psum_tensor("rlp", [64, 512], F32) as rlp:
                scp = [scpP[:, 0:512], scpP[:, 512:1024], kvps[:, 0:512], kvps[:, 512:1024]]
                avp = [avpA, avpB]

                PE4, ACT4, DVE4 = P_pe.n, P_act.n, P_dve.n
                n_u = 2 * n_qc
                per_pair_act = 4 + n_u * n_wt
                per_pair_dve = 1 + n_u

                _pe_pos = {}
                _ctr = [PE4]

                def _reg(label):
                    _ctr[0] += 1
                    _pe_pos[label] = _ctr[0]
                for _hp in range(n_pairs):
                    if _hp > 0:
                        _reg(("rlb", _hp - 1, n_u - 1))
                    _reg(("kt", _hp))
                    _reg(("vt", _hp))
                    _reg(("vtr", _hp))
                    _reg(("q", _hp, 0))
                    _reg(("q", _hp, 1))
                    for _u in range(n_u):
                        for _kt in range(4):
                            _reg(("sc", _hp, _u, _kt))
                        if _u > 0:
                            _reg(("rlb", _hp, _u - 1))
                        for _kt in range(n_wt):
                            _reg(("av", _hp, _u, _kt))
                            if _kt + 4 < n_wt:
                                _reg(("sc", _hp, _u, _kt + 4))
                _reg(("rlb", n_pairs - 1, n_u - 1))
                PE4_END = _ctr[0]

                def pe4_kt(hp): return _pe_pos[("kt", hp)]
                def pe4_vt(hp): return _pe_pos[("vt", hp)]
                def pe4_vtr(hp): return _pe_pos[("vtr", hp)]
                def pe4_q(hp, qch): return _pe_pos[("q", hp, qch)]
                def pe4_sc(hp, u, kt): return _pe_pos[("sc", hp, u, kt)]
                def pe4_av(hp, u, kt): return _pe_pos[("av", hp, u, kt)]
                def pe4_rlb(hp, u): return _pe_pos[("rlb", hp, u)]

                def act4_ktev(hp): return ACT4 + hp * per_pair_act + 1
                def act4_vtev(hp): return ACT4 + hp * per_pair_act + 2
                def act4_qev(hp, qch): return ACT4 + hp * per_pair_act + 3 + qch
                def act4_exp(hp, u, kt): return ACT4 + hp * per_pair_act + 4 + u * n_wt + kt + 1

                _dve_pos = {}
                _dctr = [DVE4]

                def _dreg(label):
                    _dctr[0] += 1
                    _dve_pos[label] = _dctr[0]
                for _hp in range(n_pairs):
                    _dreg(("vev", _hp))
                    for _u in range(n_u):
                        _dreg(("unit", _hp, _u))
                DVE4_END = _dctr[0]

                def dve4_vev(hp): return _dve_pos[("vev", hp)]
                def dve4_unit(hp, u): return _dve_pos[("unit", hp, u)]

                with nc.Block() as blk:
                    @blk.gpsimd
                    def _(gp):
                        for pi in range(2):
                            ap = bass.AP(vsb[pi], 64, [[n_wt * 2 * 65, 128], [65, n_wt * 2], [1, 1]])
                            gp.memset(ap, 1.0)
                        for hp in range(n_pairs):
                            pi = hp % 2
                            if hp >= 2:
                                gp.wait_ge(P_pe.h, pe4_av(hp - 2, n_u - 1, n_wt - 1))
                            for wsrc, off in ((wq, 0), (wk, 1024), (wv, 2048)):
                                src = bass.AP(wsrc.tensor, hp * 128,
                                              [[1024, 128], [131072, NS], [1, 128]])
                                dst = bass.AP(wpb[pi], off, [[3 * 1024, 128], [128, NS], [1, 128]])
                                gp.dma_start(out=dst, in_=src).then_inc(s_wat[pi].h, 16)
                                s_wat[pi].n += 16
                        srcwo = bass.AP(wo.tensor, 0, [[1024, 128], [131072, NS], [1, 1024]])
                        gp.dma_start(out=wo_sb[:], in_=srcwo).then_inc(s_c2.h, 16)
                        s_c2.n += 16

                    @blk.tensor
                    def _(te):
                        wat = [0, 0]

                        def _emit_rlb(rhp, ru):
                            rupi = ru % 2
                            te.wait_ge(s_lc.h, rhp * n_u + ru + 1)
                            if ru >= 1:
                                te.wait_ge(P_dve.h, dve4_unit(rhp, ru - 1))
                            elif rhp >= 1:
                                te.wait_ge(P_dve.h, dve4_unit(rhp - 1, n_u - 1))
                            m = te.matmul(rlp[:, :], ones_f[:, 0:64], rlrow[rupi][:, :],
                                          start=True, stop=True)
                            P_pe.inc(m, 1)
                        emit_rlb = [_emit_rlb]

                        for hp in range(n_pairs):
                            pi = hp % 2
                            wat[pi] += 48
                            te.wait_ge(s_wat[pi].h, wat[pi])
                            if hp > 0:
                                emit_rlb[0](hp - 1, n_u - 1)
                            if hp >= 1:
                                te.wait_ge(P_act.h, act4_qev(hp - 1, 1))
                            for kc in range(n_win // 512):
                                for hs in range(NS):
                                    m = te.matmul(kvps[:, kc * 512:(kc + 1) * 512],
                                                  wpb[pi][:, 1024 + hs * 128:1024 + (hs + 1) * 128],
                                                  xnT_win[:, hs * n_win + kc * 512:hs * n_win + (kc + 1) * 512],
                                                  start=(hs == 0), stop=(hs == NS - 1))
                            P_pe.inc(m, 1)
                            for kc in range(n_win // 512):
                                for hs in range(NS):
                                    m = te.matmul(scpP[:, kc * 512:(kc + 1) * 512],
                                                  wpb[pi][:, 2048 + hs * 128:2048 + (hs + 1) * 128],
                                                  xnT_win[:, hs * n_win + kc * 512:hs * n_win + (kc + 1) * 512],
                                                  start=(hs == 0), stop=(hs == NS - 1))
                            P_pe.inc(m, 1)
                            te.wait_ge(P_act.h, act4_ktev(hp))
                            te.wait_ge(P_act.h, act4_vtev(hp))
                            if hp >= 1:
                                te.wait_ge(P_dve.h, dve4_vev(hp - 1))
                            for kt in range(n_wt):
                                m = te.transpose(trb2[:, kt * 128:(kt + 1) * 128],
                                                 vtb[pi][:, kt * 128:(kt + 1) * 128], eyebf_sb[:])
                            P_pe.inc(m, 1)
                            for qch in range(2):
                                if qch == 1:
                                    te.wait_ge(P_act.h, act4_qev(hp, 0))
                                for qc in range(n_qc // 2):
                                    col = qch * (n_qc // 2) + qc
                                    for hs in range(NS):
                                        m = te.matmul(kvps[:, qc * 512:(qc + 1) * 512],
                                                      wpb[pi][:, hs * 128:(hs + 1) * 128],
                                                      xnT_own[:, hs * n_tok + col * 512:hs * n_tok + (col + 1) * 512],
                                                      start=(hs == 0), stop=(hs == NS - 1))
                                P_pe.inc(m, 1)
                            te.wait_ge(P_act.h, act4_qev(hp, 1))
                            te.wait_ge(P_dve.h, dve4_vev(hp))
                            for u in range(n_u):
                                hsub = u // n_qc
                                qc = u % n_qc
                                p0 = hsub * 64
                                upi = u % 2
                                if u >= 2:
                                    te.wait_ge(P_dve.h, dve4_unit(hp, u - 2))
                                elif hp >= 1:
                                    te.wait_ge(P_dve.h, dve4_unit(hp - 1, n_u - 2 + u))
                                def emit_sc(kt):
                                    kpi = kt % 4
                                    if kt >= 4:
                                        te.wait_ge(P_act.h, act4_exp(hp, u, kt - 4))
                                    elif u >= 1:
                                        te.wait_ge(P_act.h, act4_exp(hp, u - 1, n_wt - 4 + kt))
                                    elif hp >= 1:
                                        te.wait_ge(P_act.h, act4_exp(hp - 1, n_u - 1, n_wt - 4 + kt))
                                    m = te.matmul(scp[kpi][:, :],
                                                  kTb[pi][p0:p0 + 64, kt * 128:(kt + 1) * 128],
                                                  qTb[pi][p0:p0 + 64, qc * 512:(qc + 1) * 512],
                                                  start=True, stop=True)
                                    P_pe.inc(m, 1)

                                def emit_av(kt):
                                    kpi = kt % 4
                                    te.wait_ge(P_act.h, act4_exp(hp, u, kt))
                                    m = te.matmul(avp[upi][0:65, :],
                                                  vsb[pi][:, (kt * 2 + hsub) * 65:(kt * 2 + hsub) * 65 + 65],
                                                  pbuf[kpi][:, :],
                                                  start=(kt == 0), stop=(kt == n_wt - 1))
                                    P_pe.inc(m, 1)

                                for kt in range(4):
                                    emit_sc(kt)
                                if u > 0:
                                    emit_rlb[0](hp, u - 1)
                                for kt in range(n_wt):
                                    emit_av(kt)
                                    if kt + 4 < n_wt:
                                        emit_sc(kt + 4)
                        emit_rlb[0](n_pairs - 1, n_u - 1)
                        assert P_pe.n == PE4_END

                    @blk.scalar
                    def _(s):
                        for hp in range(n_pairs):
                            pi = hp % 2
                            s.wait_ge(P_pe.h, pe4_kt(hp))
                            P_act.inc(s.activation(out=kTb[pi][:], in_=kvps[:, 0:n_win],
                                                   func=AF.Identity, bias=bk_sb[:, hp:hp + 1]), 1)
                            s.wait_ge(P_pe.h, pe4_vt(hp))
                            P_act.inc(s.activation(out=vtb[pi][:], in_=scpP[:, 0:n_win],
                                                   func=AF.Identity, bias=bv_sb[:, hp:hp + 1]), 1)
                            for qch in range(2):
                                s.wait_ge(P_pe.h, pe4_q(hp, qch))
                                P_act.inc(s.activation(
                                    out=qTb[pi][:, qch * (n_tok // 2):(qch + 1) * (n_tok // 2)],
                                    in_=kvps[:, 0:n_tok // 2], func=AF.Identity,
                                    bias=bq_sb[:, hp:hp + 1]), 1)
                            for u in range(n_u):
                                hsub = u // n_qc
                                h = 2 * hp + hsub
                                for kt in range(n_wt):
                                    kpi = kt % 4
                                    s.wait_ge(P_pe.h, pe4_sc(hp, u, kt))
                                    if kt >= 4:
                                        s.wait_ge(P_pe.h, pe4_av(hp, u, kt - 4))
                                    elif u >= 1:
                                        s.wait_ge(P_pe.h, pe4_av(hp, u - 1, n_wt - 4 + kt))
                                    elif hp >= 1:
                                        s.wait_ge(P_pe.h, pe4_av(hp - 1, n_u - 1, n_wt - 4 + kt))
                                    P_act.inc(s.activation(out=pbuf[kpi][:, :], in_=scp[kpi][:, :],
                                                           func=AF.Exp,
                                                           bias=alibi_sb[:, h * 8 + kt:h * 8 + kt + 1],
                                                           scale=0.125), 1)
                                s.wait_ge(P_pe.h, pe4_av(hp, u, n_wt - 1))
                                s_lc.inc(s.activation(out=rlrow[u % 2][:, :],
                                                      in_=avp[u % 2][64:65, :],
                                                      func=AF.Copy), 1)
                        assert P_act.n == ACT4 + n_pairs * per_pair_act

                    @blk.vector
                    def _(v):
                        def finish_unit(fhp, fu):
                            fhsub = fu // n_qc
                            fqc = fu % n_qc
                            fupi = fu % 2
                            v.wait_ge(P_pe.h, pe4_rlb(fhp, fu))
                            v.reciprocal(rlbc[fupi][:, :], rlp[:, :])
                            v.drain()
                            i = v.tensor_tensor(
                                out=attnT[fhsub * 64:fhsub * 64 + 64,
                                          fhp * n_tok + fqc * 512:fhp * n_tok + (fqc + 1) * 512],
                                in0=avp[fupi][0:64, :], in1=rlbc[fupi][:, :], op=ALU.mult)
                            P_dve.inc(i, 1)

                        for hp in range(n_pairs):
                            pi = hp % 2
                            v.wait_ge(P_pe.h, pe4_vtr(hp))
                            for hsub in range(2):
                                dst = bass.AP(vsb[pi], hsub * 65,
                                              [[n_wt * 2 * 65, 128], [130, n_wt], [1, 64]])
                                src = bass.AP(trb2, hsub * 64, [[1024, 128], [128, n_wt], [1, 64]])
                                i = v.tensor_copy(dst, src)
                            P_dve.inc(i, 1)
                            for u in range(n_u):
                                finish_unit(hp, u)
                        assert P_dve.n == DVE4_END

                DVE4b, PE4b = P_dve.n, P_pe.n

                def pe_wo(t): return PE4b + 2 * t + 2
                def dve_wo(t): return DVE4b + t + 1

                with nc.Block() as blk:
                    @blk.tensor
                    def _(te):
                        te.wait_ge(s_c2.h, s_c2.n)
                        for t in range(n_tt):
                            pi = t % 2
                            if t >= 2:
                                te.wait_ge(P_dve.h, dve_wo(t - 2))
                            for dmh in range(2):
                                ps = scpP[:, pi * 512:(pi + 1) * 512] if dmh == 0 else avp[pi][:, :]
                                for sl in range(n_pairs):
                                    m = te.matmul(ps,
                                                  attnT[:, sl * n_tok + t * 128:sl * n_tok + (t + 1) * 128],
                                                  wo_sb[:, sl * D + dmh * 512:sl * D + (dmh + 1) * 512],
                                                  start=(sl == 0), stop=(sl == n_pairs - 1))
                                P_pe.inc(m, 1)

                    @blk.vector
                    def _(v):
                        for t in range(n_tt):
                            pi = t % 2
                            v.wait_ge(P_pe.h, pe_wo(t))
                            if t >= 2:
                                v.wait_ge(s_ss[pi].h, 16 * ((t - 2) // 2 + 1))
                            v.tensor_tensor(out=sstage[pi][:, 0:512],
                                            in0=scpP[:, pi * 512:(pi + 1) * 512],
                                            in1=bo_bc[:, 0:512], op=ALU.add)
                            i = v.tensor_tensor(out=sstage[pi][:, 512:1024], in0=avp[pi][:, :],
                                                in1=bo_bc[:, 512:1024], op=ALU.add)
                            P_dve.inc(i, 1)

                    @blk.sync
                    def _(sy):
                        for t in range(n_tt):
                            pi = t % 2
                            sy.wait_ge(P_dve.h, dve_wo(t))
                            sy.dma_start(out=s_scr[t * 128:(t + 1) * 128, :], in_=sstage[pi][:]) \
                              .then_inc(s_ss[pi].h, 16)
                            s_ss[pi].n += 16

        with ExitStack() as sc2:
            out2 = sc2.enter_context(nc.sbuf_tensor("out2", [128, n_tt * D], F32))
            sc2a = ExitStack()
            hwT = sc2a.enter_context(nc.sbuf_tensor("hwT", [128, NS * n_tok], BF16))
            w1r = [sc2a.enter_context(nc.sbuf_tensor(f"w1r{i}", [128, NS * 128], BF16)) for i in range(6)]
            w2_sb = sc2a.enter_context(nc.sbuf_tensor("w2_sb", [128, NS * F], BF16))
            t1b = [sc2a.enter_context(nc.sbuf_tensor(f"t1b{i}", [128, 512], F32)) for i in range(2)]
            t2b = [sc2a.enter_context(nc.sbuf_tensor(f"t2b{i}", [128, 512], F32)) for i in range(2)]
            ub = [sc2a.enter_context(nc.sbuf_tensor(f"ub{i}", [128, 512], F32)) for i in range(2)]
            hps = [sc2a.enter_context(nc.psum_tensor(f"hps{i}", [128, 512], F32)) for i in range(8)]

            PEc = P_pe.n
            with nc.Block() as blk:
                @blk.tensor
                def _(te):
                    k = 0
                    for t in range(n_tt):
                        for dmh in range(2):
                            k += 1
                            if k > 4:
                                te.wait_ge(P_dve.h, P_dve.n + k - 4)
                            P_pe.inc(te.matmul(hps[(k - 1) % 4][:, :],
                                               combT[:, t * 128:(t + 1) * 128],
                                               b2_sb[:, dmh * 512:(dmh + 1) * 512],
                                               start=True, stop=True), 1)

                @blk.vector
                def _(v):
                    k = 0
                    for t in range(n_tt):
                        for dmh in range(2):
                            k += 1
                            v.wait_ge(P_pe.h, PEc + k)
                            c = v.tensor_copy(out2[:, t * D + dmh * 512:t * D + (dmh + 1) * 512],
                                              hps[(k - 1) % 4][:, :])
                            P_dve.inc(c, 1)

            PE5, DVE5, ACT5 = P_pe.n, P_dve.n, P_act.n
            sw_units = NS * n_half * n_tc
            h1_incs = NS * n_half * 2 * n_tc
            w2_incs = n_tt * 2
            pe_per_e = h1_incs + w2_incs
            dve_per_e = 2 * sw_units + w2_incs
            act_per_e = sw_units

            def pe5_h1(e, j, half, gl, tc):
                return PE5 + e * pe_per_e + ((j * n_half + half) * 2 + gl) * n_tc + tc + 1

            def pe5_w2(e, t, dmh):
                return PE5 + e * pe_per_e + h1_incs + t * 2 + dmh + 1

            def dve5_mid(e, j, half, tc):
                return DVE5 + e * dve_per_e + ((j * n_half + half) * n_tc + tc) * 2 + 1

            def dve5_sw(e, j, half, tc):
                return DVE5 + e * dve_per_e + ((j * n_half + half) * n_tc + tc) * 2 + 2

            def dve5_w2(e, t, dmh):
                return DVE5 + e * dve_per_e + 2 * sw_units + t * 2 + dmh + 1

            def act5_silu(e, j, half, tc):
                return ACT5 + e * act_per_e + (j * n_half + half) * n_tc + tc + 1

            def w1_slot(e, j, gl):
                return (e * 16 + 2 * j + gl) % 6

            def w1_consumed(e, j, gl):
                return pe5_h1(e, j, 1, gl, n_tc - 1)

            with nc.Block() as blk:
                @blk.gpsimd
                def _(gp):
                    for e in range(n_e):
                        for j in range(NS):
                            for gl in range(2):
                                idx = e * 16 + 2 * j + gl
                                if idx >= 6:
                                    pidx = idx - 6
                                    pe_, rem = divmod(pidx, 16)
                                    pj, pgl = rem // 2, rem % 2
                                    gp.wait_ge(P_pe.h, w1_consumed(pe_, pj, pgl))
                                slot = w1_slot(e, j, gl)
                                blkid = gl * 8 + j
                                gp.dma_start(out=w1r[slot][:], in_=w1h[e, blkid, :, :]) \
                                  .then_inc(s_w1[slot].h, 16)
                                s_w1[slot].n += 16
                        if e == 0:
                            gp.dma_start(out=w2_sb[:], in_=w2h[0, :, :]).then_inc(s_w2.h, 16)
                        else:
                            gp.wait_ge(P_pe.h, pe5_w2(e - 1, n_tt - 1, 1))
                            gp.dma_start(out=w2_sb[:], in_=w2h[e, :, :]).then_inc(s_w2.h, 16)
                        s_w2.n += 16

                @blk.tensor
                def _(te):
                    w1cnt = [0] * 6
                    for e in range(n_e):
                        for j in range(NS):
                            for gl in range(2):
                                w1cnt[w1_slot(e, j, gl)] += 16
                            for half in range(n_half):
                                gidx = j * n_half + half
                                grp = gidx % 2
                                if gidx >= 2:
                                    pg = gidx - 2
                                    te.wait_ge(P_dve.h, dve5_sw(e, pg // n_half, pg % n_half, n_tc - 1))
                                elif e >= 1:
                                    if grp == 0:
                                        te.wait_ge(P_dve.h, dve5_w2(e - 1, n_tt - 1, 1))
                                    else:
                                        te.wait_ge(P_dve.h, dve5_sw(e - 1, NS - 1, n_half - 1, n_tc - 1))
                                for gl in range(2):
                                    slot = w1_slot(e, j, gl)
                                    te.wait_ge(s_w1[slot].h, w1cnt[slot])
                                    for hs in range(NS):
                                        for tc in range(n_tc):
                                            bank = grp * 4 + gl * n_tc + tc
                                            tcol = half * (n_tc * 512) + tc * 512
                                            m = te.matmul(
                                                hps[bank][:, :],
                                                w1r[slot][:, hs * 128:(hs + 1) * 128],
                                                tfT[:, hs * n_tok + tcol:hs * n_tok + tcol + 512],
                                                start=(hs == 0), stop=(hs == NS - 1))
                                            if hs == NS - 1:
                                                P_pe.inc(m, 1)
                        te.wait_ge(s_w2.h, 16 * (e + 1))
                        for t in range(n_tt):
                            if t == 0:
                                te.wait_ge(P_dve.h, dve5_sw(e, NS - 1, 0, n_tc - 1))
                            elif t == n_tt // 2:
                                te.wait_ge(P_dve.h, dve5_sw(e, NS - 1, n_half - 1, n_tc - 1))
                            for dmh in range(2):
                                if t >= 2:
                                    te.wait_ge(P_dve.h, dve5_w2(e, t - 2, dmh))
                                elif e >= 1:
                                    te.wait_ge(P_dve.h, dve5_w2(e - 1, n_tt - 2 + t, dmh))
                            mm_last = [None, None]
                            for fs in range(NS):
                                for dmh in range(2):
                                    bank = (t % 2) * 2 + dmh
                                    mm_last[dmh] = te.matmul(
                                        hps[bank][:, :],
                                        hwT[:, fs * n_tok + t * 128:fs * n_tok + (t + 1) * 128],
                                        w2_sb[:, fs * F + dmh * 512:fs * F + (dmh + 1) * 512],
                                        start=(fs == 0), stop=(fs == NS - 1))
                            P_pe.inc(mm_last[0], 1)
                            P_pe.inc(mm_last[1], 1)
                    assert P_pe.n == PE5 + n_e * pe_per_e

                @blk.vector
                def _(v):
                    for e in range(n_e):
                        for j in range(NS):
                            for half in range(n_half):
                                gidx = j * n_half + half
                                grp = gidx % 2
                                for tc in range(n_tc):
                                    gbank = grp * 4 + 0 * n_tc + tc
                                    lbank = grp * 4 + 1 * n_tc + tc
                                    tcol = half * (n_tc * 512) + tc * 512
                                    pi = (gidx * n_tc + tc) % 2
                                    v.wait_ge(P_pe.h, pe5_h1(e, j, half, 0, tc))
                                    i = v.tensor_scalar(out=t1b[pi][:], in0=hps[gbank][:, :],
                                                        scalar1=b1_sb[:, e * 16 + j:e * 16 + j + 1],
                                                        scalar2=LIMIT, op0=ALU.add, op1=ALU.min)
                                    P_dve.inc(i, 1)
                                    v.wait_ge(P_pe.h, pe5_h1(e, j, half, 1, tc))
                                    v.tensor_scalar(out=t2b[pi][:], in0=hps[lbank][:, :],
                                                    scalar1=b1_sb[:, e * 16 + 8 + j:e * 16 + 8 + j + 1],
                                                    scalar2=LIMIT, op0=ALU.add, op1=ALU.min)
                                    v.drain()
                                    v.tensor_scalar(out=t2b[pi][:], in0=t2b[pi][:], scalar1=-LIMIT,
                                                    scalar2=1.0, op0=ALU.max, op1=ALU.add)
                                    v.wait_ge(P_act.h, act5_silu(e, j, half, tc))
                                    v.drain()
                                    i = v.tensor_tensor(
                                        out=hwT[:, j * n_tok + tcol:j * n_tok + tcol + 512],
                                        in0=ub[pi][:], in1=t2b[pi][:], op=ALU.mult)
                                    P_dve.inc(i, 1)
                        for t in range(n_tt):
                            for dmh in range(2):
                                bank = (t % 2) * 2 + dmh
                                v.wait_ge(P_pe.h, pe5_w2(e, t, dmh))
                                dst = out2[:, t * D + dmh * 512:t * D + (dmh + 1) * 512]
                                i = v.scalar_tensor_tensor(out=dst, in0=hps[bank][:, :],
                                                           scalar=comb_all[:, t * E + e:t * E + e + 1],
                                                           in1=dst, op0=ALU.mult, op1=ALU.add)
                                P_dve.inc(i, 1)
                        v.drain()
                    assert P_dve.n == DVE5 + n_e * dve_per_e

                @blk.scalar
                def _(s):
                    ucount = 0
                    units = []
                    for e in range(n_e):
                        for j in range(NS):
                            for half in range(n_half):
                                for tc in range(n_tc):
                                    gidx = j * n_half + half
                                    pi = (gidx * n_tc + tc) % 2
                                    units.append((e, j, half, tc))
                                    s.wait_ge(P_dve.h, dve5_mid(e, j, half, tc))
                                    if ucount >= 2:
                                        pe_, pj, ph, ptc = units[ucount - 2]
                                        s.wait_ge(P_dve.h, dve5_sw(pe_, pj, ph, ptc))
                                    P_act.inc(s.activation(out=ub[pi][:], in_=t1b[pi][:],
                                                           func=AF.Silu, scale=ALPHA), 1)
                                    ucount += 1

            sc2a.close()

            xb2 = [sc2.enter_context(nc.sbuf_tensor(f"xb2{i}", [128, D], F32)) for i in range(4)]
            slb = [sc2.enter_context(nc.sbuf_tensor(f"slb{i}", [128, D], BF16)) for i in range(4)]
            xn32 = [sc2.enter_context(nc.sbuf_tensor(f"xn32{i}", [128, D], F32)) for i in range(4)]
            f1b = [sc2.enter_context(nc.sbuf_tensor(f"f1b{i}", [128, D], F32)) for i in range(4)]
            gb2 = [sc2.enter_context(nc.sbuf_tensor(f"gb2{i}", [128, D], F32)) for i in range(4)]
            ytb = [sc2.enter_context(nc.sbuf_tensor(f"ytb{i}", [128, D], F32)) for i in range(4)]

            DVE6, ACT6 = P_dve.n, P_act.n
            SX6B = [s_x6[i].n for i in range(4)]
            SL6B = [s_sl6[i].n for i in range(4)]

            def dve6_f1(t): return DVE6 + 2 * t + 1
            def dve6_y(t): return DVE6 + 2 * t + 2
            def act6_g(t): return ACT6 + t + 1

            with nc.Block() as blk:
                @blk.sync
                def _(sy):
                    sy.wait_ge(s_ss[0].h, s_ss[0].n)
                    sy.wait_ge(s_ss[1].h, s_ss[1].n)

                    def load(t):
                        sl4 = t % 4
                        sy.dma_start(out=xb2[sl4][:], in_=xo[t * 128:(t + 1) * 128, :]) \
                          .then_inc(s_x6[sl4].h, 16)
                        s_x6[sl4].n += 16
                        sy.dma_start(out=slb[sl4][:], in_=s_scr[t * 128:(t + 1) * 128, :]) \
                          .then_inc(s_sl6[sl4].h, 16)
                        s_sl6[sl4].n += 16

                    for t0 in range(min(4, n_tt)):
                        load(t0)
                    for t in range(n_tt):
                        pi = t % 4
                        sy.wait_ge(P_dve.h, dve6_y(t))
                        sy.dma_start(out=yout[t * 128:(t + 1) * 128, :], in_=ytb[pi][:]) \
                          .then_inc(s_y[t % 2].h, 16)
                        s_y[t % 2].n += 16
                        if t + 4 < n_tt:
                            load(t + 4)
                    sy.wait_ge(s_y[0].h, s_y[0].n)
                    sy.wait_ge(s_y[1].h, s_y[1].n)

                @blk.vector
                def _(v):
                    for t in range(n_tt):
                        pi = t % 4
                        si = t % 2
                        v.wait_ge(s_x6[pi].h, SX6B[pi] + 16 * (t // 4 + 1))
                        v.tensor_scalar(out=xn32[pi][:], in0=xb2[pi][:], scalar1=mall[:, t:t + 1],
                                        scalar2=rstdall[:, t:t + 1], op0=ALU.subtract, op1=ALU.mult)
                        v.drain()
                        i = v.tensor_tensor(out=f1b[pi][:], in0=xn32[pi][:],
                                            in1=out2[:, t * D:(t + 1) * D], op=ALU.add)
                        P_dve.inc(i, 1)
                        v.wait_ge(P_act.h, act6_g(t))
                        v.wait_ge(s_sl6[pi].h, SL6B[pi] + 16 * (t // 4 + 1))
                        if t >= 2:
                            v.wait_ge(s_y[si].h, 16 * ((t - 2) // 2 + 1))
                        v.tensor_tensor(out=ytb[pi][:], in0=gb2[pi][:], in1=slb[pi][:], op=ALU.mult)
                        v.drain()
                        i = v.tensor_tensor(out=ytb[pi][:], in0=ytb[pi][:], in1=xb2[pi][:], op=ALU.add)
                        P_dve.inc(i, 1)

                @blk.scalar
                def _(s):
                    for t in range(n_tt):
                        pi = t % 4
                        s.wait_ge(P_dve.h, dve6_f1(t))
                        if t >= 4:
                            s.wait_ge(P_dve.h, dve6_y(t - 4))
                        P_act.inc(s.activation(out=gb2[pi][:], in_=f1b[pi][:], func=AF.Silu), 1)

    return nc


def prep_inputs(inputs, n_e=32):
    x = np.ascontiguousarray(inputs["L0_attn_x"], dtype=np.float32)
    out = {}
    out["wq"] = np.ascontiguousarray(inputs["L0_attn_Wq"], dtype=np.float32)
    out["wk"] = np.ascontiguousarray(inputs["L0_attn_Wk"], dtype=np.float32)
    out["wv"] = np.ascontiguousarray(inputs["L0_attn_Wv"], dtype=np.float32)
    out["wo"] = np.ascontiguousarray(inputs["L0_attn_Wo"], dtype=np.float32)
    out["bq_r"] = np.ascontiguousarray(np.asarray(inputs["L0_attn_bq"], dtype=np.float32).reshape(NS, 128).T)
    out["bk_r"] = np.ascontiguousarray(np.asarray(inputs["L0_attn_bk"], dtype=np.float32).reshape(NS, 128).T)
    out["bv_r"] = np.ascontiguousarray(np.asarray(inputs["L0_attn_bv"], dtype=np.float32).reshape(NS, 128).T)
    out["bo_row"] = np.ascontiguousarray(np.asarray(inputs["L0_attn_bo"], dtype=np.float32).reshape(1, D))
    out["nsc_row"] = np.ascontiguousarray(np.asarray(inputs["L0_ffn_norm_scale"], dtype=np.float32).reshape(1, D))
    gw = np.asarray(inputs["L0_ffn_gate_w"], dtype=np.float32)
    out["gwT_r"] = np.ascontiguousarray(gw.T.reshape(NS, 128, E).transpose(1, 0, 2).reshape(128, NS * E))
    out["gb_row"] = np.ascontiguousarray(np.asarray(inputs["L0_ffn_gate_b"], dtype=np.float32).reshape(1, E))
    w1 = np.asarray(inputs["L0_ffn_w1"], dtype=np.float32)
    w1r = np.concatenate([w1[:, 0::2, :], w1[:, 1::2, :]], axis=1)
    w1t = w1r.transpose(0, 2, 1)
    w1h = w1t.reshape(E, NS, 128, 16, 128).transpose(0, 3, 2, 1, 4)
    out["w1h"] = np.ascontiguousarray(w1h.reshape(E, 16, 128, NS * 128)[:n_e])
    b1 = np.asarray(inputs["L0_ffn_b1"], dtype=np.float32)
    b1r = np.concatenate([b1[:, 0::2], b1[:, 1::2]], axis=1)
    out["b1h"] = np.ascontiguousarray(b1r.reshape(E, 16, 128).transpose(2, 0, 1).reshape(128, E * 16)[:, :n_e * 16])
    w2 = np.asarray(inputs["L0_ffn_w2"], dtype=np.float32)
    w2t = w2.transpose(0, 2, 1)
    out["w2h"] = np.ascontiguousarray(w2t.reshape(E, NS, 128, D).transpose(0, 2, 1, 3).reshape(E, 128, NS * D)[:n_e])
    out["b2d"] = np.ascontiguousarray(np.asarray(inputs["L0_ffn_b2"], dtype=np.float32))
    slopes = 1.0 / 2.0 ** (np.arange(NH, dtype=np.float64) / NH)
    iot = np.arange(128, dtype=np.float64)
    al = np.zeros((128, NH * 8), dtype=np.float32)
    for h in range(NH):
        for kt in range(8):
            kg = (SEQ - WIN) + kt * 128 + iot
            al[:, h * 8 + kt] = (slopes[h] * (kg - (SEQ - 1)) - BOFF).astype(np.float32)
    out["alibi"] = al
    out["eye_bf"] = np.eye(128, dtype=ml_dtypes.bfloat16)
    out["eye_f32"] = np.eye(128, dtype=np.float32)
    return x, out


def kernel(**inputs):
    x, shared = prep_inputs(inputs)
    in_maps = []
    for c in range(8):
        b = c // 4
        q0 = (c % 4) * TPC
        m = dict(shared)
        m["xo"] = np.ascontiguousarray(x[b, q0:q0 + TPC, :])
        m["xw"] = np.ascontiguousarray(x[b, SEQ - WIN:, :])
        in_maps.append(m)
    nc = build_program()
    res = run_bass_kernel_spmd(nc, in_maps, core_ids=list(range(8)))
    y = np.empty((2, SEQ, D), dtype=np.float32)
    for c in range(8):
        b = c // 4
        q0 = (c % 4) * TPC
        y[b, q0:q0 + TPC, :] = np.asarray(res.results[c]["yout"]).reshape(TPC, D)
    return y


# version 19
# speedup vs baseline: 1.0068x; 1.0068x over previous
import numpy as np
import ml_dtypes
from contextlib import ExitStack

import concourse.bass as bass
import concourse.mybir as mybir
from concourse.bass_utils import run_bass_kernel_spmd

F32 = mybir.dt.float32
BF16 = mybir.dt.bfloat16
AF = mybir.ActivationFunctionType
ALU = mybir.AluOpType
AX = mybir.AxisListType

D = 1024
NS = 8
HD = 64
NH = 16
TPC = 2048
WIN = 1024
EPS = 1e-5
ALPHA = 1.702
LIMIT = 7.0
BOFF = 20.0
SEQ = 8192
E = 32
F = 1024

CFG = dict(n_tt=16, n_wt=8, n_pairs=8, n_e=32)


class Sem:
    def __init__(self, es, nc, name):
        self.h = es.enter_context(nc.semaphore(name))
        self.n = 0

    def inc(self, instr, k=1):
        instr.then_inc(self.h, k)
        self.n += k
        return self.n


def build_program(cfg=None):
    cfg = dict(CFG, **(cfg or {}))
    n_tt, n_wt, n_pairs, n_e = cfg["n_tt"], cfg["n_wt"], cfg["n_pairs"], cfg["n_e"]
    n_tok = n_tt * 128
    n_win = n_wt * 128
    n_qc = n_tok // 512
    n_half = 2
    n_tc = n_tok // 512 // n_half
    assert n_tc >= 1 and n_qc >= 2

    nc = bass.Bass("TRN2", target_bir_lowering=False)

    xo = nc.dram_tensor("xo", [n_tok, D], F32, kind="ExternalInput").ap()
    xw = nc.dram_tensor("xw", [n_win, D], F32, kind="ExternalInput").ap()
    wq = nc.dram_tensor("wq", [D, D], F32, kind="ExternalInput").ap()
    wk = nc.dram_tensor("wk", [D, D], F32, kind="ExternalInput").ap()
    wv = nc.dram_tensor("wv", [D, D], F32, kind="ExternalInput").ap()
    wo = nc.dram_tensor("wo", [D, D], F32, kind="ExternalInput").ap()
    bq_r = nc.dram_tensor("bq_r", [128, NS], F32, kind="ExternalInput").ap()
    bk_r = nc.dram_tensor("bk_r", [128, NS], F32, kind="ExternalInput").ap()
    bv_r = nc.dram_tensor("bv_r", [128, NS], F32, kind="ExternalInput").ap()
    bo_row = nc.dram_tensor("bo_row", [1, D], F32, kind="ExternalInput").ap()
    nsc_row = nc.dram_tensor("nsc_row", [1, D], F32, kind="ExternalInput").ap()
    gwT_r = nc.dram_tensor("gwT_r", [128, NS * E], F32, kind="ExternalInput").ap()
    gb_row = nc.dram_tensor("gb_row", [1, E], F32, kind="ExternalInput").ap()
    w1h = nc.dram_tensor("w1h", [n_e, 16, 128, NS * 128], F32, kind="ExternalInput").ap()
    b1h = nc.dram_tensor("b1h", [128, n_e * 16], F32, kind="ExternalInput").ap()
    w2h = nc.dram_tensor("w2h", [n_e, 128, NS * F], F32, kind="ExternalInput").ap()
    b2d = nc.dram_tensor("b2d", [E, F], F32, kind="ExternalInput").ap()
    alibi = nc.dram_tensor("alibi", [128, NH * 8], F32, kind="ExternalInput").ap()
    eye_bf = nc.dram_tensor("eye_bf", [128, 128], BF16, kind="ExternalInput").ap()
    eye_f32 = nc.dram_tensor("eye_f32", [128, 128], F32, kind="ExternalInput").ap()
    s_scr = nc.dram_tensor("s_scr", [n_tok, D], BF16, kind="Internal").ap()
    yout = nc.dram_tensor("yout", [n_tok, D], F32, kind="ExternalOutput").ap()

    es = ExitStack()
    with es:
        tfT = es.enter_context(nc.sbuf_tensor("tfT", [128, NS * n_tok], BF16))
        combT = es.enter_context(nc.sbuf_tensor("combT", [E, n_tok], BF16))
        comb_all = es.enter_context(nc.sbuf_tensor("comb_all", [128, CFG['n_tt'] * E], F32))
        g_all = es.enter_context(nc.sbuf_tensor("g_all", [128, n_tt * E], F32))
        mall = es.enter_context(nc.sbuf_tensor("mall", [128, n_tt], F32))
        rstdall = es.enter_context(nc.sbuf_tensor("rstdall", [128, n_tt], F32))
        bq_sb = es.enter_context(nc.sbuf_tensor("bq_sb", [128, NS], F32))
        bk_sb = es.enter_context(nc.sbuf_tensor("bk_sb", [128, NS], F32))
        bv_sb = es.enter_context(nc.sbuf_tensor("bv_sb", [128, NS], F32))
        gwT_sb = es.enter_context(nc.sbuf_tensor("gwT_sb", [128, NS * E], F32))
        alibi_sb = es.enter_context(nc.sbuf_tensor("alibi_sb", [128, NH * 8], F32))
        eyebf_sb = es.enter_context(nc.sbuf_tensor("eyebf_sb", [128, 128], BF16))
        eyef_sb = es.enter_context(nc.sbuf_tensor("eyef_sb", [128, 128], F32))
        ones_f = es.enter_context(nc.sbuf_tensor("ones_f", [1, 128], F32))
        cbones = es.enter_context(nc.sbuf_tensor("cbones", [1, 128], BF16))
        nsc_bc = es.enter_context(nc.sbuf_tensor("nsc_bc", [128, D], F32))
        gb_bc = es.enter_context(nc.sbuf_tensor("gb_bc", [128, E], F32))
        bo_bc = es.enter_context(nc.sbuf_tensor("bo_bc", [128, D], F32))
        b1_sb = es.enter_context(nc.sbuf_tensor("b1_sb", [128, n_e * 16], F32))
        b2_sb = es.enter_context(nc.sbuf_tensor("b2_sb", [E, F], BF16))
        eps_sb = es.enter_context(nc.sbuf_tensor("eps_sb", [128, 1], F32))
        eps2_sb = es.enter_context(nc.sbuf_tensor("eps2_sb", [128, 1], F32))
        es0 = ExitStack()
        bo_sbr = es0.enter_context(nc.sbuf_tensor("bo_sbr", [1, D], F32))
        nsc_sbr = es0.enter_context(nc.sbuf_tensor("nsc_sbr", [1, D], F32))
        gb_sbr = es0.enter_context(nc.sbuf_tensor("gb_sbr", [1, E], F32))

        s_x = [Sem(es, nc, f"s_x{i}") for i in range(2)]
        s_sl = [Sem(es, nc, f"s_sl{i}") for i in range(2)]
        s_ss = [Sem(es, nc, f"s_ss{i}") for i in range(2)]
        s_y = [Sem(es, nc, f"s_y{i}") for i in range(2)]
        s_c1 = Sem(es, nc, "s_c1")
        s_c2 = Sem(es, nc, "s_c2")
        s_wat = [Sem(es, nc, f"s_wat{i}") for i in range(2)]
        s_w1 = [Sem(es, nc, f"s_w1{i}") for i in range(6)]
        s_w2 = Sem(es, nc, "s_w2")
        s_x6 = [Sem(es, nc, f"s_x6{i}") for i in range(4)]
        s_sl6 = [Sem(es, nc, f"s_sl6{i}") for i in range(4)]
        P_pe = Sem(es, nc, "P_pe")
        P_dve = Sem(es, nc, "P_dve")
        P_act = Sem(es, nc, "P_act")
        s_lc = Sem(es, nc, "s_lc")

        with nc.Block() as blk:
            @blk.sync
            def _(sy):
                for dst, src in ((bq_sb, bq_r), (bk_sb, bk_r), (bv_sb, bv_r),
                                 (bo_sbr, bo_row), (nsc_sbr, nsc_row), (gb_sbr, gb_row),
                                 (alibi_sb, alibi), (eyebf_sb, eye_bf), (eyef_sb, eye_f32),
                                 (gwT_sb, gwT_r), (b1_sb, b1h)):
                    sy.dma_start(out=dst[:], in_=src[:, :]).then_inc(s_c1.h, 16)
                    s_c1.n += 16

            @blk.gpsimd
            def _(gp):
                gp.memset(ones_f[:], 1.0)
                gp.memset(cbones[:], 1.0 / ALPHA)
                gp.memset(eps_sb[:], EPS)
                gp.memset(eps2_sb[:], EPS * EPS)
                gp.dma_start(out=b2_sb[:], in_=b2d[:, :]).then_inc(s_c2.h, 16)
                s_c2.n += 16

            @blk.vector
            def _(v):
                v.wait_ge(s_c1.h, s_c1.n)
                v.wait_ge(s_c2.h, s_c2.n)

        with nc.psum_tensor("pbc2", [128, 1024], F32) as pbc2, \
             nc.psum_tensor("pbc3", [128, 1024], F32) as pbc3, \
             nc.psum_tensor("pbc4", [128, E], F32) as pbc4:
            with nc.Block() as blk:
                @blk.tensor
                def _(te):
                    te.matmul(pbc2[:, 0:512], ones_f[:, :], bo_sbr[:, 0:512], start=True, stop=True)
                    te.matmul(pbc2[:, 512:1024], ones_f[:, :], bo_sbr[:, 512:1024], start=True, stop=True)
                    te.matmul(pbc3[:, 0:512], ones_f[:, :], nsc_sbr[:, 0:512], start=True, stop=True)
                    te.matmul(pbc3[:, 512:1024], ones_f[:, :], nsc_sbr[:, 512:1024], start=True, stop=True)
                    m = te.matmul(pbc4[:, :], ones_f[:, :], gb_sbr[:, :], start=True, stop=True)
                    P_pe.inc(m, 1)

                @blk.vector
                def _(v):
                    v.wait_ge(P_pe.h, P_pe.n)
                    v.tensor_copy(bo_bc[:], pbc2[:, :])
                    v.tensor_copy(nsc_bc[:], pbc3[:, :])
                    c = v.tensor_copy(gb_bc[:], pbc4[:, :])
                    P_dve.inc(c, 1)

                @blk.scalar
                def _(s):
                    s.wait_ge(P_dve.h, P_dve.n)

        es0.close()

        with ExitStack() as sc1:
            xnT_own = sc1.enter_context(nc.sbuf_tensor("xnT_own", [128, NS * n_tok], BF16))
            xnT_win = sc1.enter_context(nc.sbuf_tensor("xnT_win", [128, NS * n_win], BF16))
            sc1a = ExitStack()
            xb = [sc1a.enter_context(nc.sbuf_tensor(f"xb{i}", [128, D], F32)) for i in range(4)]
            st6 = [sc1a.enter_context(nc.sbuf_tensor(f"st6{i}", [128, 12], F32)) for i in range(4)]
            mv = [sc1a.enter_context(nc.sbuf_tensor(f"mv{i}", [128, 2], F32)) for i in range(4)]
            sdb = [sc1a.enter_context(nc.sbuf_tensor(f"sdb{i}", [128, 2], F32)) for i in range(4)]
            rstdb = [sc1a.enter_context(nc.sbuf_tensor(f"rstdb{i}", [128, 2], F32)) for i in range(4)]
            xnbf = [sc1a.enter_context(nc.sbuf_tensor(f"xnbf{i}", [128, D], BF16)) for i in range(4)]
            tf32 = [sc1a.enter_context(nc.sbuf_tensor(f"tf32{i}", [128, D], F32)) for i in range(4)]
            tfTt = [sc1a.enter_context(nc.sbuf_tensor(f"tfTt{i}", [128, D], F32)) for i in range(4)]
            gTsb = sc1a.enter_context(nc.sbuf_tensor("gTsb", [E, 128], F32))

            with nc.psum_tensor("trbf0", [128, 1024], BF16) as trbfA, \
                 nc.psum_tensor("trbf1", [128, 1024], BF16) as trbfB, \
                 nc.psum_tensor("tr320", [128, 1024], F32) as tr32A, \
                 nc.psum_tensor("tr321", [128, 1024], F32) as tr32B, \
                 nc.psum_tensor("gTp", [E, 128], F32) as gTp, \
                 nc.psum_tensor("g2p", [128, E], F32) as g2p:
                trbf = [trbfA, trbfB]
                tr32 = [tr32A, tr32B]

                DVE0, ACT0, PE0 = P_dve.n, P_act.n, P_pe.n
                SX0 = [s_x[0].n, s_x[1].n]

                _d1_pos = {}
                _d1c = [DVE0]

                def _d1reg(lbl):
                    _d1c[0] += 1
                    _d1_pos[lbl] = _d1c[0]
                _d1reg(("A", 0)); _d1reg(("C", 0)); _d1reg(("D", 0))
                for _t in range(n_tt):
                    if _t + 1 < n_tt:
                        _d1reg(("A", _t + 1))
                    _d1reg(("E", _t)); _d1reg(("F", _t))
                    if _t + 1 < n_tt:
                        _d1reg(("C", _t + 1)); _d1reg(("D", _t + 1))
                    _d1reg(("M", _t)); _d1reg(("G", _t))
                assert _d1c[0] == DVE0 + 7 * n_tt

                def dveA(t): return _d1_pos[("A", t)]
                def dveC(t): return _d1_pos[("C", t)]
                def dveD(t): return _d1_pos[("D", t)]
                def dveE(t): return _d1_pos[("E", t)]
                def dveF(t): return _d1_pos[("F", t)]
                def dveGm(t): return _d1_pos[("M", t)]
                def dveG(t): return _d1_pos[("G", t)]
                def actSD(t): return ACT0 + 2 * t + 2
                def peTB(t): return PE0 + 4 * t + 1
                def peT32(t): return PE0 + 4 * t + 2
                def peGm(t): return PE0 + 4 * t + 3
                def peG(t): return PE0 + 4 * t + 4

                with nc.Block() as blk:
                    @blk.sync
                    def _(sy):
                        for t in range(n_tt):
                            pi = t % 4
                            if t >= 4:
                                sy.wait_ge(P_dve.h, dveD(t - 4))
                            sy.dma_start(out=xb[pi][:], in_=xo[t * 128:(t + 1) * 128, :]) \
                              .then_inc(s_x6[pi].h, 16)
                            s_x6[pi].n += 16

                    @blk.vector
                    def _(v):
                        SX60 = [s_x6[i].n - 16 * ((n_tt + 3 - i) // 4) for i in range(4)]

                        def stats(t):
                            pi = t % 4
                            v.wait_ge(s_x6[pi].h, SX60[pi] + 16 * (t // 4 + 1))
                            if t >= 4:
                                v.wait_ge(P_act.h, actSD(t - 4))
                            v.bn_stats(st6[pi][:, 0:6], xb[pi][:, 0:512])
                            v.bn_stats(st6[pi][:, 6:12], xb[pi][:, 512:1024])
                            v.drain()
                            i = v.bn_aggr(mv[pi][:], st6[pi][:])
                            P_dve.inc(i, 1)

                        def norms(t):
                            pi = t % 4
                            v.wait_ge(P_act.h, actSD(t))
                            v.reciprocal(rstdb[pi][:], sdb[pi][:])
                            v.drain()
                            v.tensor_copy(mall[:, t:t + 1], mv[pi][:, 0:1])
                            v.tensor_copy(rstdall[:, t:t + 1], rstdb[pi][:, 0:1])
                            i = v.tensor_scalar(out=xnbf[pi][:], in0=xb[pi][:], scalar1=mv[pi][:, 0:1],
                                                scalar2=rstdb[pi][:, 0:1], op0=ALU.subtract, op1=ALU.mult)
                            P_dve.inc(i, 1)
                            v.drain()
                            v.tensor_scalar(out=tf32[pi][:], in0=xb[pi][:], scalar1=mv[pi][:, 0:1],
                                            scalar2=rstdb[pi][:, 1:2], op0=ALU.subtract, op1=ALU.mult)
                            v.drain()
                            i = v.tensor_tensor(out=tf32[pi][:], in0=tf32[pi][:], in1=nsc_bc[:], op=ALU.mult)
                            P_dve.inc(i, 1)

                        def ef_evacs(t):
                            pi = t % 4
                            v.wait_ge(P_pe.h, peTB(t))
                            dst = bass.AP(xnT_own, t * 128, [[NS * n_tok, 128], [n_tok, NS], [1, 128]])
                            src = bass.AP(trbf[t % 2], 0, [[1024, 128], [128, NS], [1, 128]])
                            i = v.tensor_copy(dst, src)
                            P_dve.inc(i, 1)
                            v.wait_ge(P_pe.h, peT32(t))
                            v.tensor_copy(tfTt[pi][:], tr32[t % 2][:, :])
                            dst2 = bass.AP(tfT, t * 128, [[NS * n_tok, 128], [n_tok, NS], [1, 128]])
                            i = v.tensor_copy(dst2, tr32[t % 2][:, :])
                            P_dve.inc(i, 1)

                        def gm_evacs(t):
                            v.wait_ge(P_pe.h, peGm(t))
                            i = v.tensor_copy(gTsb[:, :], gTp[:, :])
                            P_dve.inc(i, 1)
                            v.wait_ge(P_pe.h, peG(t))
                            i = v.tensor_tensor(out=g_all[:, t * E:(t + 1) * E], in0=g2p[:, :],
                                                in1=gb_bc[:], op=ALU.add)
                            P_dve.inc(i, 1)

                        stats(0)
                        norms(0)
                        for t in range(n_tt):
                            if t + 1 < n_tt:
                                stats(t + 1)
                            ef_evacs(t)
                            if t + 1 < n_tt:
                                norms(t + 1)
                            gm_evacs(t)
                        assert P_dve.n == DVE0 + 7 * n_tt

                    @blk.scalar
                    def _(s):
                        for t in range(n_tt):
                            pi = t % 4
                            s.wait_ge(P_dve.h, dveA(t))
                            P_act.inc(s.activation(out=sdb[pi][:, 0:1], in_=mv[pi][:, 1:2],
                                                   func=AF.Sqrt, bias=eps_sb[:]), 1)
                            P_act.inc(s.activation(out=sdb[pi][:, 1:2], in_=mv[pi][:, 1:2],
                                                   func=AF.Sqrt, scale=1.0 + EPS,
                                                   bias=eps2_sb[:]), 1)
                        assert P_act.n == ACT0 + 2 * n_tt

                    @blk.tensor
                    def _(te):
                        for t in range(n_tt):
                            pi = t % 4
                            te.wait_ge(P_dve.h, dveC(t))
                            if t >= 2:
                                te.wait_ge(P_dve.h, dveE(t - 2))
                            for hs in range(NS):
                                m = te.transpose(trbf[t % 2][:, hs * 128:(hs + 1) * 128],
                                                 xnbf[pi][:, hs * 128:(hs + 1) * 128], eyebf_sb[:])
                            P_pe.inc(m, 1)
                            te.wait_ge(P_dve.h, dveD(t))
                            if t >= 2:
                                te.wait_ge(P_dve.h, dveF(t - 2))
                            for hs in range(NS):
                                m = te.transpose(tr32[t % 2][:, hs * 128:(hs + 1) * 128],
                                                 tf32[pi][:, hs * 128:(hs + 1) * 128], eyef_sb[:])
                            P_pe.inc(m, 1)
                            te.wait_ge(P_dve.h, dveF(t))
                            if t >= 1:
                                te.wait_ge(P_dve.h, dveGm(t - 1))
                            for hs in range(NS):
                                m = te.matmul(gTp[:, :], gwT_sb[:, hs * E:(hs + 1) * E],
                                              tfTt[pi][:, hs * 128:(hs + 1) * 128],
                                              start=(hs == 0), stop=(hs == NS - 1))
                            P_pe.inc(m, 1)
                            te.wait_ge(P_dve.h, dveGm(t))
                            if t >= 1:
                                te.wait_ge(P_dve.h, dveG(t - 1))
                            m = te.transpose(g2p[:, :], gTsb[:, :], eyef_sb[0:32, 0:32])
                            P_pe.inc(m, 1)
                        assert P_pe.n == PE0 + 4 * n_tt

                DVE1, ACT1, PE1 = P_dve.n, P_act.n, P_pe.n
                SX1 = [s_x[0].n, s_x[1].n]

                def dveWA(t): return DVE1 + 3 * t + 1
                def dveWC(t): return DVE1 + 3 * t + 2
                def dveWE(t): return DVE1 + 3 * t + 3
                def actWSD(t): return ACT1 + t + 1
                def peWTB(t): return PE1 + t + 1

                with nc.Block() as blk:
                    @blk.sync
                    def _(sy):
                        for t in range(n_wt):
                            pi = t % 4
                            if t >= 4:
                                sy.wait_ge(P_dve.h, dveWC(t - 4))
                            sy.dma_start(out=xb[pi][:], in_=xw[t * 128:(t + 1) * 128, :]) \
                              .then_inc(s_x6[pi].h, 16)
                            s_x6[pi].n += 16

                    @blk.vector
                    def _(v):
                        SX61 = [s_x6[i].n - 16 * ((n_wt + 3 - i) // 4) for i in range(4)]
                        for t in range(n_wt):
                            pi = t % 4
                            v.wait_ge(s_x6[pi].h, SX61[pi] + 16 * (t // 4 + 1))
                            if t >= 4:
                                v.wait_ge(P_act.h, actWSD(t - 4))
                            v.bn_stats(st6[pi][:, 0:6], xb[pi][:, 0:512])
                            v.bn_stats(st6[pi][:, 6:12], xb[pi][:, 512:1024])
                            v.drain()
                            i = v.bn_aggr(mv[pi][:], st6[pi][:])
                            P_dve.inc(i, 1)
                            v.wait_ge(P_act.h, actWSD(t))
                            v.reciprocal(rstdb[pi][:, 0:1], sdb[pi][:, 0:1])
                            v.drain()
                            i = v.tensor_scalar(out=xnbf[pi][:], in0=xb[pi][:], scalar1=mv[pi][:, 0:1],
                                                scalar2=rstdb[pi][:, 0:1], op0=ALU.subtract, op1=ALU.mult)
                            P_dve.inc(i, 1)
                            v.wait_ge(P_pe.h, peWTB(t))
                            dst = bass.AP(xnT_win, t * 128, [[NS * n_win, 128], [n_win, NS], [1, 128]])
                            src = bass.AP(trbf[t % 2], 0, [[1024, 128], [128, NS], [1, 128]])
                            i = v.tensor_copy(dst, src)
                            P_dve.inc(i, 1)

                    @blk.scalar
                    def _(s):
                        for t in range(n_wt):
                            pi = t % 4
                            s.wait_ge(P_dve.h, dveWA(t))
                            P_act.inc(s.activation(out=sdb[pi][:, 0:1], in_=mv[pi][:, 1:2],
                                                   func=AF.Sqrt, bias=eps_sb[:]), 1)

                    @blk.tensor
                    def _(te):
                        for t in range(n_wt):
                            pi = t % 4
                            te.wait_ge(P_dve.h, dveWC(t))
                            if t >= 2:
                                te.wait_ge(P_dve.h, dveWE(t - 2))
                            for hs in range(NS):
                                m = te.transpose(trbf[t % 2][:, hs * 128:(hs + 1) * 128],
                                                 xnbf[pi][:, hs * 128:(hs + 1) * 128], eyebf_sb[:])
                            P_pe.inc(m, 1)

            m8b = [sc1a.enter_context(nc.sbuf_tensor(f"m8b{i}", [128, 8], F32)) for i in range(2)]
            nmx = [sc1a.enter_context(nc.sbuf_tensor(f"nmx{i}", [128, 1], F32)) for i in range(2)]
            maskb = [sc1a.enter_context(nc.sbuf_tensor(f"maskb{i}", [128, E], F32)) for i in range(2)]
            expb = [sc1a.enter_context(nc.sbuf_tensor(f"expb{i}", [128, E], F32)) for i in range(2)]
            denb = [sc1a.enter_context(nc.sbuf_tensor(f"denb{i}", [128, 1], F32)) for i in range(2)]
            combb = [sc1a.enter_context(nc.sbuf_tensor(f"combb{i}", [128, E], F32)) for i in range(2)]

            DVE2, ACT2, PE2 = P_dve.n, P_act.n, P_pe.n

            def dveH(t): return DVE2 + 3 * t + 1
            def dveJ(t): return DVE2 + 3 * t + 2
            def dveL(t): return DVE2 + 3 * t + 3
            def actI(t): return ACT2 + t + 1
            def peK(t): return PE2 + t + 1

            with nc.psum_tensor("ptr0", [E, 128], F32) as ptr0, \
                 nc.psum_tensor("ptr1", [E, 128], F32) as ptr1:
                ptrs = [ptr0, ptr1]
                with nc.Block() as blk:
                    @blk.vector
                    def _(v):
                        for t in range(n_tt):
                            pi = t % 2
                            gsl = g_all[:, t * E:(t + 1) * E]
                            if t >= 2:
                                v.wait_ge(P_act.h, actI(t - 2))
                            v.max(m8b[pi][:], gsl)
                            v.drain()
                            v.tensor_scalar(out=nmx[pi][:], in0=m8b[pi][:, 0:1], scalar1=-1.0,
                                            scalar2=None, op0=ALU.mult)
                            i = v.tensor_scalar(out=maskb[pi][:], in0=gsl, scalar1=m8b[pi][:, 3:4],
                                                scalar2=None, op0=ALU.is_ge)
                            P_dve.inc(i, 1)
                            v.wait_ge(P_act.h, actI(t))
                            v.tensor_tensor(out=expb[pi][:], in0=expb[pi][:], in1=maskb[pi][:],
                                            op=ALU.mult)
                            v.drain()
                            v.reduce_sum(denb[pi][:], expb[pi][:], axis=AX.X)
                            v.drain()
                            v.reciprocal(denb[pi][:], denb[pi][:])
                            v.drain()
                            i = v.tensor_scalar(out=combb[pi][:], in0=expb[pi][:], scalar1=denb[pi][:],
                                                scalar2=None, op0=ALU.mult)
                            P_dve.inc(i, 1)
                            v.drain()
                            v.tensor_scalar(out=comb_all[:, t * E:(t + 1) * E], in0=combb[pi][:],
                                            scalar1=1.0 / ALPHA, scalar2=None, op0=ALU.mult)
                            v.wait_ge(P_pe.h, peK(t))
                            i = v.tensor_copy(combT[:, t * 128:(t + 1) * 128], ptrs[pi][:, :])
                            P_dve.inc(i, 1)

                    @blk.scalar
                    def _(s):
                        for t in range(n_tt):
                            pi = t % 2
                            s.wait_ge(P_dve.h, dveH(t))
                            P_act.inc(s.activation(out=expb[pi][:], in_=g_all[:, t * E:(t + 1) * E],
                                                   func=AF.Exp, bias=nmx[pi][:]), 1)

                    @blk.tensor
                    def _(te):
                        for t in range(n_tt):
                            pi = t % 2
                            te.wait_ge(P_dve.h, dveJ(t))
                            if t >= 2:
                                te.wait_ge(P_dve.h, dveL(t - 2))
                            P_pe.inc(te.transpose(ptrs[pi][:, :], combb[pi][:, :], eyef_sb[:]), 1)

            sc1a.close()

            wo_sb = sc1.enter_context(nc.sbuf_tensor("wo_sb", [128, NS * D], BF16))
            attnT = sc1.enter_context(nc.sbuf_tensor("attnT", [128, n_pairs * n_tok], BF16))
            wpb = [sc1.enter_context(nc.sbuf_tensor(f"wpb{i}", [128, 3 * 1024], BF16)) for i in range(2)]
            kTb = [sc1.enter_context(nc.sbuf_tensor(f"kTb{i}", [128, n_win], BF16)) for i in range(2)]
            vtb = [sc1.enter_context(nc.sbuf_tensor(f"vtb{i}", [128, n_win], BF16)) for i in range(2)]
            vsb = [sc1.enter_context(nc.sbuf_tensor(f"vsb{i}", [128, n_wt * 2 * 65], BF16)) for i in range(2)]
            qTb = [sc1.enter_context(nc.sbuf_tensor(f"qTb{i}", [128, n_tok], BF16)) for i in range(2)]
            pbuf = [sc1.enter_context(nc.sbuf_tensor(f"pbuf{i}", [128, 512], BF16)) for i in range(4)]
            rlrow = [sc1.enter_context(nc.sbuf_tensor(f"rlrow{i}", [1, 512], F32)) for i in range(2)]
            rlbc = [sc1.enter_context(nc.sbuf_tensor(f"rlbc{i}", [64, 512], F32)) for i in range(2)]
            sstage = [sc1.enter_context(nc.sbuf_tensor(f"sstage{i}", [128, D], BF16)) for i in range(2)]

            with nc.psum_tensor("kvps", [128, 1024], F32) as kvps, \
                 nc.psum_tensor("trb2", [128, 1024], BF16) as trb2, \
                 nc.psum_tensor("scpP", [128, 1024], F32) as scpP, \
                 nc.psum_tensor("avpA", [128, 512], F32) as avpA, \
                 nc.psum_tensor("avpB", [128, 512], F32) as avpB, \
                 nc.psum_tensor("rlp", [64, 512], F32) as rlp:
                scp = [scpP[:, 0:512], scpP[:, 512:1024], kvps[:, 0:512], kvps[:, 512:1024]]
                avp = [avpA, avpB]

                PE4, ACT4, DVE4 = P_pe.n, P_act.n, P_dve.n
                n_u = 2 * n_qc
                per_pair_act = 4 + n_u * n_wt
                per_pair_dve = 1 + n_u

                _pe_pos = {}
                _ctr = [PE4]

                def _reg(label):
                    _ctr[0] += 1
                    _pe_pos[label] = _ctr[0]
                for _hp in range(n_pairs):
                    if _hp > 0:
                        _reg(("rlb", _hp - 1, n_u - 1))
                    _reg(("kt", _hp))
                    _reg(("vt", _hp))
                    _reg(("vtr", _hp))
                    _reg(("q", _hp, 0))
                    _reg(("q", _hp, 1))
                    for _u in range(n_u):
                        for _kt in range(4):
                            _reg(("sc", _hp, _u, _kt))
                        if _u > 0:
                            _reg(("rlb", _hp, _u - 1))
                        for _kt in range(n_wt):
                            _reg(("av", _hp, _u, _kt))
                            if _kt + 4 < n_wt:
                                _reg(("sc", _hp, _u, _kt + 4))
                _reg(("rlb", n_pairs - 1, n_u - 1))
                PE4_END = _ctr[0]

                def pe4_kt(hp): return _pe_pos[("kt", hp)]
                def pe4_vt(hp): return _pe_pos[("vt", hp)]
                def pe4_vtr(hp): return _pe_pos[("vtr", hp)]
                def pe4_q(hp, qch): return _pe_pos[("q", hp, qch)]
                def pe4_sc(hp, u, kt): return _pe_pos[("sc", hp, u, kt)]
                def pe4_av(hp, u, kt): return _pe_pos[("av", hp, u, kt)]
                def pe4_rlb(hp, u): return _pe_pos[("rlb", hp, u)]

                def act4_ktev(hp): return ACT4 + hp * per_pair_act + 1
                def act4_vtev(hp): return ACT4 + hp * per_pair_act + 2
                def act4_qev(hp, qch): return ACT4 + hp * per_pair_act + 3 + qch
                def act4_exp(hp, u, kt): return ACT4 + hp * per_pair_act + 4 + u * n_wt + kt + 1

                _dve_pos = {}
                _dctr = [DVE4]

                def _dreg(label):
                    _dctr[0] += 1
                    _dve_pos[label] = _dctr[0]
                for _hp in range(n_pairs):
                    _dreg(("vev", _hp))
                    for _u in range(n_u):
                        _dreg(("unit", _hp, _u))
                DVE4_END = _dctr[0]

                def dve4_vev(hp): return _dve_pos[("vev", hp)]
                def dve4_unit(hp, u): return _dve_pos[("unit", hp, u)]

                with nc.Block() as blk:
                    @blk.gpsimd
                    def _(gp):
                        for pi in range(2):
                            ap = bass.AP(vsb[pi], 64, [[n_wt * 2 * 65, 128], [65, n_wt * 2], [1, 1]])
                            gp.memset(ap, 1.0)
                        for hp in range(n_pairs):
                            pi = hp % 2
                            if hp >= 2:
                                gp.wait_ge(P_pe.h, pe4_av(hp - 2, n_u - 1, n_wt - 1))
                            for wsrc, off in ((wq, 0), (wk, 1024), (wv, 2048)):
                                src = bass.AP(wsrc.tensor, hp * 128,
                                              [[1024, 128], [131072, NS], [1, 128]])
                                dst = bass.AP(wpb[pi], off, [[3 * 1024, 128], [128, NS], [1, 128]])
                                gp.dma_start(out=dst, in_=src).then_inc(s_wat[pi].h, 16)
                                s_wat[pi].n += 16
                        srcwo = bass.AP(wo.tensor, 0, [[1024, 128], [131072, NS], [1, 1024]])
                        gp.dma_start(out=wo_sb[:], in_=srcwo).then_inc(s_c2.h, 16)
                        s_c2.n += 16

                    @blk.tensor
                    def _(te):
                        wat = [0, 0]

                        def _emit_rlb(rhp, ru):
                            rupi = ru % 2
                            te.wait_ge(s_lc.h, rhp * n_u + ru + 1)
                            if ru >= 1:
                                te.wait_ge(P_dve.h, dve4_unit(rhp, ru - 1))
                            elif rhp >= 1:
                                te.wait_ge(P_dve.h, dve4_unit(rhp - 1, n_u - 1))
                            m = te.matmul(rlp[:, :], ones_f[:, 0:64], rlrow[rupi][:, :],
                                          start=True, stop=True)
                            P_pe.inc(m, 1)
                        emit_rlb = [_emit_rlb]

                        for hp in range(n_pairs):
                            pi = hp % 2
                            wat[pi] += 48
                            te.wait_ge(s_wat[pi].h, wat[pi])
                            if hp > 0:
                                emit_rlb[0](hp - 1, n_u - 1)
                            if hp >= 1:
                                te.wait_ge(P_act.h, act4_qev(hp - 1, 1))
                            for kc in range(n_win // 512):
                                for hs in range(NS):
                                    m = te.matmul(kvps[:, kc * 512:(kc + 1) * 512],
                                                  wpb[pi][:, 1024 + hs * 128:1024 + (hs + 1) * 128],
                                                  xnT_win[:, hs * n_win + kc * 512:hs * n_win + (kc + 1) * 512],
                                                  start=(hs == 0), stop=(hs == NS - 1))
                            P_pe.inc(m, 1)
                            for kc in range(n_win // 512):
                                for hs in range(NS):
                                    m = te.matmul(scpP[:, kc * 512:(kc + 1) * 512],
                                                  wpb[pi][:, 2048 + hs * 128:2048 + (hs + 1) * 128],
                                                  xnT_win[:, hs * n_win + kc * 512:hs * n_win + (kc + 1) * 512],
                                                  start=(hs == 0), stop=(hs == NS - 1))
                            P_pe.inc(m, 1)
                            te.wait_ge(P_act.h, act4_ktev(hp))
                            te.wait_ge(P_act.h, act4_vtev(hp))
                            if hp >= 1:
                                te.wait_ge(P_dve.h, dve4_vev(hp - 1))
                            for kt in range(n_wt):
                                m = te.transpose(trb2[:, kt * 128:(kt + 1) * 128],
                                                 vtb[pi][:, kt * 128:(kt + 1) * 128], eyebf_sb[:])
                            P_pe.inc(m, 1)
                            for qch in range(2):
                                if qch == 1:
                                    te.wait_ge(P_act.h, act4_qev(hp, 0))
                                for qc in range(n_qc // 2):
                                    col = qch * (n_qc // 2) + qc
                                    for hs in range(NS):
                                        m = te.matmul(kvps[:, qc * 512:(qc + 1) * 512],
                                                      wpb[pi][:, hs * 128:(hs + 1) * 128],
                                                      xnT_own[:, hs * n_tok + col * 512:hs * n_tok + (col + 1) * 512],
                                                      start=(hs == 0), stop=(hs == NS - 1))
                                P_pe.inc(m, 1)
                            te.wait_ge(P_act.h, act4_qev(hp, 1))
                            te.wait_ge(P_dve.h, dve4_vev(hp))
                            for u in range(n_u):
                                hsub = u // n_qc
                                qc = u % n_qc
                                p0 = hsub * 64
                                upi = u % 2
                                if u >= 2:
                                    te.wait_ge(P_dve.h, dve4_unit(hp, u - 2))
                                elif hp >= 1:
                                    te.wait_ge(P_dve.h, dve4_unit(hp - 1, n_u - 2 + u))
                                def emit_sc(kt):
                                    kpi = kt % 4
                                    if kt >= 4:
                                        te.wait_ge(P_act.h, act4_exp(hp, u, kt - 4))
                                    elif u >= 1:
                                        te.wait_ge(P_act.h, act4_exp(hp, u - 1, n_wt - 4 + kt))
                                    elif hp >= 1:
                                        te.wait_ge(P_act.h, act4_exp(hp - 1, n_u - 1, n_wt - 4 + kt))
                                    m = te.matmul(scp[kpi][:, :],
                                                  kTb[pi][p0:p0 + 64, kt * 128:(kt + 1) * 128],
                                                  qTb[pi][p0:p0 + 64, qc * 512:(qc + 1) * 512],
                                                  start=True, stop=True)
                                    P_pe.inc(m, 1)

                                def emit_av(kt):
                                    kpi = kt % 4
                                    te.wait_ge(P_act.h, act4_exp(hp, u, kt))
                                    m = te.matmul(avp[upi][0:65, :],
                                                  vsb[pi][:, (kt * 2 + hsub) * 65:(kt * 2 + hsub) * 65 + 65],
                                                  pbuf[kpi][:, :],
                                                  start=(kt == 0), stop=(kt == n_wt - 1))
                                    P_pe.inc(m, 1)

                                for kt in range(4):
                                    emit_sc(kt)
                                if u > 0:
                                    emit_rlb[0](hp, u - 1)
                                for kt in range(n_wt):
                                    emit_av(kt)
                                    if kt + 4 < n_wt:
                                        emit_sc(kt + 4)
                        emit_rlb[0](n_pairs - 1, n_u - 1)
                        assert P_pe.n == PE4_END

                    @blk.scalar
                    def _(s):
                        for hp in range(n_pairs):
                            pi = hp % 2
                            s.wait_ge(P_pe.h, pe4_kt(hp))
                            P_act.inc(s.activation(out=kTb[pi][:], in_=kvps[:, 0:n_win],
                                                   func=AF.Identity, bias=bk_sb[:, hp:hp + 1]), 1)
                            s.wait_ge(P_pe.h, pe4_vt(hp))
                            P_act.inc(s.activation(out=vtb[pi][:], in_=scpP[:, 0:n_win],
                                                   func=AF.Identity, bias=bv_sb[:, hp:hp + 1]), 1)
                            for qch in range(2):
                                s.wait_ge(P_pe.h, pe4_q(hp, qch))
                                P_act.inc(s.activation(
                                    out=qTb[pi][:, qch * (n_tok // 2):(qch + 1) * (n_tok // 2)],
                                    in_=kvps[:, 0:n_tok // 2], func=AF.Identity,
                                    bias=bq_sb[:, hp:hp + 1]), 1)
                            for u in range(n_u):
                                hsub = u // n_qc
                                h = 2 * hp + hsub
                                for kt in range(n_wt):
                                    kpi = kt % 4
                                    s.wait_ge(P_pe.h, pe4_sc(hp, u, kt))
                                    if kt >= 4:
                                        s.wait_ge(P_pe.h, pe4_av(hp, u, kt - 4))
                                    elif u >= 1:
                                        s.wait_ge(P_pe.h, pe4_av(hp, u - 1, n_wt - 4 + kt))
                                    elif hp >= 1:
                                        s.wait_ge(P_pe.h, pe4_av(hp - 1, n_u - 1, n_wt - 4 + kt))
                                    P_act.inc(s.activation(out=pbuf[kpi][:, :], in_=scp[kpi][:, :],
                                                           func=AF.Exp,
                                                           bias=alibi_sb[:, h * 8 + kt:h * 8 + kt + 1],
                                                           scale=0.125), 1)
                                s.wait_ge(P_pe.h, pe4_av(hp, u, n_wt - 1))
                                s_lc.inc(s.activation(out=rlrow[u % 2][:, :],
                                                      in_=avp[u % 2][64:65, :],
                                                      func=AF.Copy), 1)
                        assert P_act.n == ACT4 + n_pairs * per_pair_act

                    @blk.vector
                    def _(v):
                        def finish_unit(fhp, fu):
                            fhsub = fu // n_qc
                            fqc = fu % n_qc
                            fupi = fu % 2
                            v.wait_ge(P_pe.h, pe4_rlb(fhp, fu))
                            v.reciprocal(rlbc[fupi][:, :], rlp[:, :])
                            v.drain()
                            i = v.tensor_tensor(
                                out=attnT[fhsub * 64:fhsub * 64 + 64,
                                          fhp * n_tok + fqc * 512:fhp * n_tok + (fqc + 1) * 512],
                                in0=avp[fupi][0:64, :], in1=rlbc[fupi][:, :], op=ALU.mult)
                            P_dve.inc(i, 1)

                        for hp in range(n_pairs):
                            pi = hp % 2
                            v.wait_ge(P_pe.h, pe4_vtr(hp))
                            for hsub in range(2):
                                dst = bass.AP(vsb[pi], hsub * 65,
                                              [[n_wt * 2 * 65, 128], [130, n_wt], [1, 64]])
                                src = bass.AP(trb2, hsub * 64, [[1024, 128], [128, n_wt], [1, 64]])
                                i = v.tensor_copy(dst, src)
                            P_dve.inc(i, 1)
                            for u in range(n_u):
                                finish_unit(hp, u)
                        assert P_dve.n == DVE4_END

                DVE4b, PE4b = P_dve.n, P_pe.n

                def pe_wo(t): return PE4b + 2 * t + 2
                def dve_wo(t): return DVE4b + t + 1

                with nc.Block() as blk:
                    @blk.tensor
                    def _(te):
                        te.wait_ge(s_c2.h, s_c2.n)
                        for t in range(n_tt):
                            pi = t % 2
                            if t >= 2:
                                te.wait_ge(P_dve.h, dve_wo(t - 2))
                            for dmh in range(2):
                                ps = scpP[:, pi * 512:(pi + 1) * 512] if dmh == 0 else avp[pi][:, :]
                                for sl in range(n_pairs):
                                    m = te.matmul(ps,
                                                  attnT[:, sl * n_tok + t * 128:sl * n_tok + (t + 1) * 128],
                                                  wo_sb[:, sl * D + dmh * 512:sl * D + (dmh + 1) * 512],
                                                  start=(sl == 0), stop=(sl == n_pairs - 1))
                                P_pe.inc(m, 1)

                    @blk.vector
                    def _(v):
                        for t in range(n_tt):
                            pi = t % 2
                            v.wait_ge(P_pe.h, pe_wo(t))
                            if t >= 2:
                                v.wait_ge(s_ss[pi].h, 16 * ((t - 2) // 2 + 1))
                            v.tensor_tensor(out=sstage[pi][:, 0:512],
                                            in0=scpP[:, pi * 512:(pi + 1) * 512],
                                            in1=bo_bc[:, 0:512], op=ALU.add)
                            i = v.tensor_tensor(out=sstage[pi][:, 512:1024], in0=avp[pi][:, :],
                                                in1=bo_bc[:, 512:1024], op=ALU.add)
                            P_dve.inc(i, 1)

                    @blk.sync
                    def _(sy):
                        for t in range(n_tt):
                            pi = t % 2
                            sy.wait_ge(P_dve.h, dve_wo(t))
                            sy.dma_start(out=s_scr[t * 128:(t + 1) * 128, :], in_=sstage[pi][:]) \
                              .then_inc(s_ss[pi].h, 16)
                            s_ss[pi].n += 16

        with ExitStack() as sc2:
            out2 = sc2.enter_context(nc.sbuf_tensor("out2", [128, n_tt * D], F32))
            sc2a = ExitStack()
            hwT = sc2a.enter_context(nc.sbuf_tensor("hwT", [128, NS * n_tok], BF16))
            w1r = [sc2a.enter_context(nc.sbuf_tensor(f"w1r{i}", [128, NS * 128], BF16)) for i in range(6)]
            w2_sb = sc2a.enter_context(nc.sbuf_tensor("w2_sb", [128, NS * F], BF16))
            t1b = [sc2a.enter_context(nc.sbuf_tensor(f"t1b{i}", [128, 512], F32)) for i in range(2)]
            t2b = [sc2a.enter_context(nc.sbuf_tensor(f"t2b{i}", [128, 512], F32)) for i in range(2)]
            ub = [sc2a.enter_context(nc.sbuf_tensor(f"ub{i}", [128, 512], F32)) for i in range(2)]
            hps = [sc2a.enter_context(nc.psum_tensor(f"hps{i}", [128, 512], F32)) for i in range(8)]

            PEc = P_pe.n
            with nc.Block() as blk:
                @blk.tensor
                def _(te):
                    k = 0
                    for t in range(n_tt):
                        for dmh in range(2):
                            k += 1
                            if k > 4:
                                te.wait_ge(P_dve.h, P_dve.n + k - 4)
                            P_pe.inc(te.matmul(hps[(k - 1) % 4][:, :],
                                               combT[:, t * 128:(t + 1) * 128],
                                               b2_sb[:, dmh * 512:(dmh + 1) * 512],
                                               start=True, stop=True), 1)

                @blk.vector
                def _(v):
                    k = 0
                    for t in range(n_tt):
                        for dmh in range(2):
                            k += 1
                            v.wait_ge(P_pe.h, PEc + k)
                            c = v.tensor_copy(out2[:, t * D + dmh * 512:t * D + (dmh + 1) * 512],
                                              hps[(k - 1) % 4][:, :])
                            P_dve.inc(c, 1)

            PE5, DVE5, ACT5 = P_pe.n, P_dve.n, P_act.n
            sw_units = NS * n_half * n_tc
            h1_incs = NS * n_half * 2 * n_tc
            w2_incs = n_tt * 2
            pe_per_e = h1_incs + w2_incs
            dve_per_e = 2 * sw_units + w2_incs
            act_per_e = sw_units

            def pe5_h1(e, j, half, gl, tc):
                return PE5 + e * pe_per_e + ((j * n_half + half) * 2 + gl) * n_tc + tc + 1

            def pe5_w2(e, t, dmh):
                return PE5 + e * pe_per_e + h1_incs + t * 2 + dmh + 1

            def dve5_mid(e, j, half, tc):
                return DVE5 + e * dve_per_e + ((j * n_half + half) * n_tc + tc) * 2 + 1

            def dve5_sw(e, j, half, tc):
                return DVE5 + e * dve_per_e + ((j * n_half + half) * n_tc + tc) * 2 + 2

            def dve5_w2(e, t, dmh):
                return DVE5 + e * dve_per_e + 2 * sw_units + t * 2 + dmh + 1

            def act5_silu(e, j, half, tc):
                return ACT5 + e * act_per_e + (j * n_half + half) * n_tc + tc + 1

            def w1_slot(e, j, gl):
                return (e * 16 + 2 * j + gl) % 6

            def w1_consumed(e, j, gl):
                return pe5_h1(e, j, 1, gl, n_tc - 1)

            with nc.Block() as blk:
                @blk.gpsimd
                def _(gp):
                    for e in range(n_e):
                        for j in range(NS):
                            for gl in range(2):
                                idx = e * 16 + 2 * j + gl
                                if idx >= 6:
                                    pidx = idx - 6
                                    pe_, rem = divmod(pidx, 16)
                                    pj, pgl = rem // 2, rem % 2
                                    gp.wait_ge(P_pe.h, w1_consumed(pe_, pj, pgl))
                                slot = w1_slot(e, j, gl)
                                blkid = gl * 8 + j
                                gp.dma_start(out=w1r[slot][:], in_=w1h[e, blkid, :, :]) \
                                  .then_inc(s_w1[slot].h, 16)
                                s_w1[slot].n += 16
                        if e == 0:
                            gp.dma_start(out=w2_sb[:], in_=w2h[0, :, :]).then_inc(s_w2.h, 16)
                        else:
                            gp.wait_ge(P_pe.h, pe5_w2(e - 1, n_tt - 1, 1))
                            gp.dma_start(out=w2_sb[:], in_=w2h[e, :, :]).then_inc(s_w2.h, 16)
                        s_w2.n += 16

                @blk.tensor
                def _(te):
                    w1cnt = [0] * 6
                    for e in range(n_e):
                        for j in range(NS):
                            for gl in range(2):
                                w1cnt[w1_slot(e, j, gl)] += 16
                            for half in range(n_half):
                                gidx = j * n_half + half
                                grp = gidx % 2
                                if gidx >= 2:
                                    pg = gidx - 2
                                    te.wait_ge(P_dve.h, dve5_sw(e, pg // n_half, pg % n_half, n_tc - 1))
                                elif e >= 1:
                                    if grp == 0:
                                        te.wait_ge(P_dve.h, dve5_w2(e - 1, n_tt - 1, 1))
                                    else:
                                        te.wait_ge(P_dve.h, dve5_sw(e - 1, NS - 1, n_half - 1, n_tc - 1))
                                for gl in range(2):
                                    slot = w1_slot(e, j, gl)
                                    te.wait_ge(s_w1[slot].h, w1cnt[slot])
                                    for hs in range(NS):
                                        for tc in range(n_tc):
                                            bank = grp * 4 + gl * n_tc + tc
                                            tcol = half * (n_tc * 512) + tc * 512
                                            m = te.matmul(
                                                hps[bank][:, :],
                                                w1r[slot][:, hs * 128:(hs + 1) * 128],
                                                tfT[:, hs * n_tok + tcol:hs * n_tok + tcol + 512],
                                                start=(hs == 0), stop=(hs == NS - 1))
                                            if hs == NS - 1:
                                                P_pe.inc(m, 1)
                        te.wait_ge(s_w2.h, 16 * (e + 1))
                        for t in range(n_tt):
                            if t == 0:
                                te.wait_ge(P_dve.h, dve5_sw(e, NS - 1, 0, n_tc - 1))
                            elif t == n_tt // 2:
                                te.wait_ge(P_dve.h, dve5_sw(e, NS - 1, n_half - 1, n_tc - 1))
                            for dmh in range(2):
                                if t >= 2:
                                    te.wait_ge(P_dve.h, dve5_w2(e, t - 2, dmh))
                                elif e >= 1:
                                    te.wait_ge(P_dve.h, dve5_w2(e - 1, n_tt - 2 + t, dmh))
                            mm_last = [None, None]
                            for fs in range(NS):
                                for dmh in range(2):
                                    bank = (t % 2) * 2 + dmh
                                    mm_last[dmh] = te.matmul(
                                        hps[bank][:, :],
                                        hwT[:, fs * n_tok + t * 128:fs * n_tok + (t + 1) * 128],
                                        w2_sb[:, fs * F + dmh * 512:fs * F + (dmh + 1) * 512],
                                        start=(fs == 0), stop=(fs == NS - 1))
                            P_pe.inc(mm_last[0], 1)
                            P_pe.inc(mm_last[1], 1)
                    assert P_pe.n == PE5 + n_e * pe_per_e

                @blk.vector
                def _(v):
                    for e in range(n_e):
                        for j in range(NS):
                            for half in range(n_half):
                                gidx = j * n_half + half
                                grp = gidx % 2
                                for tc in range(n_tc):
                                    gbank = grp * 4 + 0 * n_tc + tc
                                    lbank = grp * 4 + 1 * n_tc + tc
                                    tcol = half * (n_tc * 512) + tc * 512
                                    pi = (gidx * n_tc + tc) % 2
                                    v.wait_ge(P_pe.h, pe5_h1(e, j, half, 0, tc))
                                    i = v.tensor_scalar(out=t1b[pi][:], in0=hps[gbank][:, :],
                                                        scalar1=b1_sb[:, e * 16 + j:e * 16 + j + 1],
                                                        scalar2=LIMIT, op0=ALU.add, op1=ALU.min)
                                    P_dve.inc(i, 1)
                                    v.wait_ge(P_pe.h, pe5_h1(e, j, half, 1, tc))
                                    v.tensor_scalar(out=t2b[pi][:], in0=hps[lbank][:, :],
                                                    scalar1=b1_sb[:, e * 16 + 8 + j:e * 16 + 8 + j + 1],
                                                    scalar2=LIMIT, op0=ALU.add, op1=ALU.min)
                                    v.drain()
                                    v.tensor_scalar(out=t2b[pi][:], in0=t2b[pi][:], scalar1=-LIMIT,
                                                    scalar2=1.0, op0=ALU.max, op1=ALU.add)
                                    v.wait_ge(P_act.h, act5_silu(e, j, half, tc))
                                    v.drain()
                                    i = v.tensor_tensor(
                                        out=hwT[:, j * n_tok + tcol:j * n_tok + tcol + 512],
                                        in0=ub[pi][:], in1=t2b[pi][:], op=ALU.mult)
                                    P_dve.inc(i, 1)
                        for t in range(n_tt):
                            for dmh in range(2):
                                bank = (t % 2) * 2 + dmh
                                v.wait_ge(P_pe.h, pe5_w2(e, t, dmh))
                                dst = out2[:, t * D + dmh * 512:t * D + (dmh + 1) * 512]
                                i = v.scalar_tensor_tensor(out=dst, in0=hps[bank][:, :],
                                                           scalar=comb_all[:, t * E + e:t * E + e + 1],
                                                           in1=dst, op0=ALU.mult, op1=ALU.add)
                                P_dve.inc(i, 1)
                        v.drain()
                    assert P_dve.n == DVE5 + n_e * dve_per_e

                @blk.scalar
                def _(s):
                    ucount = 0
                    units = []
                    for e in range(n_e):
                        for j in range(NS):
                            for half in range(n_half):
                                for tc in range(n_tc):
                                    gidx = j * n_half + half
                                    pi = (gidx * n_tc + tc) % 2
                                    units.append((e, j, half, tc))
                                    s.wait_ge(P_dve.h, dve5_mid(e, j, half, tc))
                                    if ucount >= 2:
                                        pe_, pj, ph, ptc = units[ucount - 2]
                                        s.wait_ge(P_dve.h, dve5_sw(pe_, pj, ph, ptc))
                                    P_act.inc(s.activation(out=ub[pi][:], in_=t1b[pi][:],
                                                           func=AF.Silu, scale=ALPHA), 1)
                                    ucount += 1

            sc2a.close()

            xb2 = [sc2.enter_context(nc.sbuf_tensor(f"xb2{i}", [128, D], F32)) for i in range(4)]
            slb = [sc2.enter_context(nc.sbuf_tensor(f"slb{i}", [128, D], BF16)) for i in range(4)]
            xn32 = [sc2.enter_context(nc.sbuf_tensor(f"xn32{i}", [128, D], F32)) for i in range(4)]
            f1b = [sc2.enter_context(nc.sbuf_tensor(f"f1b{i}", [128, D], F32)) for i in range(4)]
            gb2 = [sc2.enter_context(nc.sbuf_tensor(f"gb2{i}", [128, D], F32)) for i in range(4)]
            ytb = [sc2.enter_context(nc.sbuf_tensor(f"ytb{i}", [128, D], F32)) for i in range(4)]

            DVE6, ACT6 = P_dve.n, P_act.n
            SX6B = [s_x6[i].n for i in range(4)]
            SL6B = [s_sl6[i].n for i in range(4)]

            def dve6_f1(t): return DVE6 + 2 * t + 1
            def dve6_y(t): return DVE6 + 2 * t + 2
            def act6_g(t): return ACT6 + t + 1

            with nc.Block() as blk:
                @blk.sync
                def _(sy):
                    sy.wait_ge(s_ss[0].h, s_ss[0].n)
                    sy.wait_ge(s_ss[1].h, s_ss[1].n)

                    def load(t):
                        sl4 = t % 4
                        sy.dma_start(out=xb2[sl4][:], in_=xo[t * 128:(t + 1) * 128, :]) \
                          .then_inc(s_x6[sl4].h, 16)
                        s_x6[sl4].n += 16
                        sy.dma_start(out=slb[sl4][:], in_=s_scr[t * 128:(t + 1) * 128, :]) \
                          .then_inc(s_sl6[sl4].h, 16)
                        s_sl6[sl4].n += 16

                    for t0 in range(min(4, n_tt)):
                        load(t0)
                    for t in range(n_tt):
                        pi = t % 4
                        sy.wait_ge(P_dve.h, dve6_y(t))
                        sy.dma_start(out=yout[t * 128:(t + 1) * 128, :], in_=ytb[pi][:]) \
                          .then_inc(s_y[t % 2].h, 16)
                        s_y[t % 2].n += 16
                        if t + 4 < n_tt:
                            load(t + 4)
                    sy.wait_ge(s_y[0].h, s_y[0].n)
                    sy.wait_ge(s_y[1].h, s_y[1].n)

                @blk.vector
                def _(v):
                    for t in range(n_tt):
                        pi = t % 4
                        si = t % 2
                        v.wait_ge(s_x6[pi].h, SX6B[pi] + 16 * (t // 4 + 1))
                        v.tensor_scalar(out=xn32[pi][:], in0=xb2[pi][:], scalar1=mall[:, t:t + 1],
                                        scalar2=rstdall[:, t:t + 1], op0=ALU.subtract, op1=ALU.mult)
                        v.drain()
                        i = v.tensor_tensor(out=f1b[pi][:], in0=xn32[pi][:],
                                            in1=out2[:, t * D:(t + 1) * D], op=ALU.add)
                        P_dve.inc(i, 1)
                        v.wait_ge(P_act.h, act6_g(t))
                        v.wait_ge(s_sl6[pi].h, SL6B[pi] + 16 * (t // 4 + 1))
                        if t >= 2:
                            v.wait_ge(s_y[si].h, 16 * ((t - 2) // 2 + 1))
                        v.tensor_tensor(out=ytb[pi][:], in0=gb2[pi][:], in1=slb[pi][:], op=ALU.mult)
                        v.drain()
                        i = v.tensor_tensor(out=ytb[pi][:], in0=ytb[pi][:], in1=xb2[pi][:], op=ALU.add)
                        P_dve.inc(i, 1)

                @blk.scalar
                def _(s):
                    for t in range(n_tt):
                        pi = t % 4
                        s.wait_ge(P_dve.h, dve6_f1(t))
                        if t >= 4:
                            s.wait_ge(P_dve.h, dve6_y(t - 4))
                        P_act.inc(s.activation(out=gb2[pi][:], in_=f1b[pi][:], func=AF.Silu), 1)

    return nc


def prep_inputs(inputs, n_e=32):
    x = np.ascontiguousarray(inputs["L0_attn_x"], dtype=np.float32)
    out = {}
    out["wq"] = np.ascontiguousarray(inputs["L0_attn_Wq"], dtype=np.float32)
    out["wk"] = np.ascontiguousarray(inputs["L0_attn_Wk"], dtype=np.float32)
    out["wv"] = np.ascontiguousarray(inputs["L0_attn_Wv"], dtype=np.float32)
    out["wo"] = np.ascontiguousarray(inputs["L0_attn_Wo"], dtype=np.float32)
    out["bq_r"] = np.ascontiguousarray(np.asarray(inputs["L0_attn_bq"], dtype=np.float32).reshape(NS, 128).T)
    out["bk_r"] = np.ascontiguousarray(np.asarray(inputs["L0_attn_bk"], dtype=np.float32).reshape(NS, 128).T)
    out["bv_r"] = np.ascontiguousarray(np.asarray(inputs["L0_attn_bv"], dtype=np.float32).reshape(NS, 128).T)
    out["bo_row"] = np.ascontiguousarray(np.asarray(inputs["L0_attn_bo"], dtype=np.float32).reshape(1, D))
    out["nsc_row"] = np.ascontiguousarray(np.asarray(inputs["L0_ffn_norm_scale"], dtype=np.float32).reshape(1, D))
    gw = np.asarray(inputs["L0_ffn_gate_w"], dtype=np.float32)
    out["gwT_r"] = np.ascontiguousarray(gw.T.reshape(NS, 128, E).transpose(1, 0, 2).reshape(128, NS * E))
    out["gb_row"] = np.ascontiguousarray(np.asarray(inputs["L0_ffn_gate_b"], dtype=np.float32).reshape(1, E))
    w1 = np.asarray(inputs["L0_ffn_w1"], dtype=np.float32)
    w1r = np.concatenate([w1[:, 0::2, :], w1[:, 1::2, :]], axis=1)
    w1t = w1r.transpose(0, 2, 1)
    w1h = w1t.reshape(E, NS, 128, 16, 128).transpose(0, 3, 2, 1, 4)
    out["w1h"] = np.ascontiguousarray(w1h.reshape(E, 16, 128, NS * 128)[:n_e])
    b1 = np.asarray(inputs["L0_ffn_b1"], dtype=np.float32)
    b1r = np.concatenate([b1[:, 0::2], b1[:, 1::2]], axis=1)
    out["b1h"] = np.ascontiguousarray(b1r.reshape(E, 16, 128).transpose(2, 0, 1).reshape(128, E * 16)[:, :n_e * 16])
    w2 = np.asarray(inputs["L0_ffn_w2"], dtype=np.float32)
    w2t = w2.transpose(0, 2, 1)
    out["w2h"] = np.ascontiguousarray(w2t.reshape(E, NS, 128, D).transpose(0, 2, 1, 3).reshape(E, 128, NS * D)[:n_e])
    out["b2d"] = np.ascontiguousarray(np.asarray(inputs["L0_ffn_b2"], dtype=np.float32))
    slopes = 1.0 / 2.0 ** (np.arange(NH, dtype=np.float64) / NH)
    iot = np.arange(128, dtype=np.float64)
    al = np.zeros((128, NH * 8), dtype=np.float32)
    for h in range(NH):
        for kt in range(8):
            kg = (SEQ - WIN) + kt * 128 + iot
            al[:, h * 8 + kt] = (slopes[h] * (kg - (SEQ - 1)) - BOFF).astype(np.float32)
    out["alibi"] = al
    out["eye_bf"] = np.eye(128, dtype=ml_dtypes.bfloat16)
    out["eye_f32"] = np.eye(128, dtype=np.float32)
    return x, out


def kernel(**inputs):
    x, shared = prep_inputs(inputs)
    in_maps = []
    for c in range(8):
        b = c // 4
        q0 = (c % 4) * TPC
        m = dict(shared)
        m["xo"] = np.ascontiguousarray(x[b, q0:q0 + TPC, :])
        m["xw"] = np.ascontiguousarray(x[b, SEQ - WIN:, :])
        in_maps.append(m)
    nc = build_program()
    res = run_bass_kernel_spmd(nc, in_maps, core_ids=list(range(8)))
    y = np.empty((2, SEQ, D), dtype=np.float32)
    for c in range(8):
        b = c // 4
        q0 = (c % 4) * TPC
        y[b, q0:q0 + TPC, :] = np.asarray(res.results[c]["yout"]).reshape(TPC, D)
    return y


# version 20 (final)
# speedup vs baseline: 1.1804x; 1.1723x over previous
import numpy as np
import ml_dtypes
from contextlib import ExitStack

import concourse.bass as bass
import concourse.mybir as mybir
from concourse.bass_utils import run_bass_kernel_spmd

F32 = mybir.dt.float32
BF16 = mybir.dt.bfloat16
AF = mybir.ActivationFunctionType
ALU = mybir.AluOpType
AX = mybir.AxisListType

D = 1024
NS = 8
HD = 64
NH = 16
TPC = 2048
WIN = 1024
EPS = 1e-5
ALPHA = 1.702
LIMIT = 7.0
BOFF = 20.0
SEQ = 8192
E = 32
F = 1024

CFG = dict(n_tt=16, n_wt=8, n_pairs=8, n_e=32)


class Sem:
    def __init__(self, es, nc, name):
        self.h = es.enter_context(nc.semaphore(name))
        self.n = 0

    def inc(self, instr, k=1):
        instr.then_inc(self.h, k)
        self.n += k
        return self.n


def build_program(cfg=None):
    cfg = dict(CFG, **(cfg or {}))
    n_tt, n_wt, n_pairs, n_e = cfg["n_tt"], cfg["n_wt"], cfg["n_pairs"], cfg["n_e"]
    n_tok = n_tt * 128
    n_win = n_wt * 128
    n_qc = n_tok // 512
    n_half = 2
    n_tc = n_tok // 512 // n_half
    assert n_tc >= 1 and n_qc >= 2

    nc = bass.Bass("TRN2", target_bir_lowering=False)

    xo = nc.dram_tensor("xo", [n_tok, D], F32, kind="ExternalInput").ap()
    xw = nc.dram_tensor("xw", [n_win, D], F32, kind="ExternalInput").ap()
    wq = nc.dram_tensor("wq", [D, D], F32, kind="ExternalInput").ap()
    wk = nc.dram_tensor("wk", [D, D], F32, kind="ExternalInput").ap()
    wv = nc.dram_tensor("wv", [D, D], F32, kind="ExternalInput").ap()
    wo = nc.dram_tensor("wo", [D, D], F32, kind="ExternalInput").ap()
    bq_r = nc.dram_tensor("bq_r", [128, NS], F32, kind="ExternalInput").ap()
    bk_r = nc.dram_tensor("bk_r", [128, NS], F32, kind="ExternalInput").ap()
    bv_r = nc.dram_tensor("bv_r", [128, NS], F32, kind="ExternalInput").ap()
    bo_row = nc.dram_tensor("bo_row", [1, D], F32, kind="ExternalInput").ap()
    nsc_row = nc.dram_tensor("nsc_row", [1, D], F32, kind="ExternalInput").ap()
    gwT_r = nc.dram_tensor("gwT_r", [128, NS * E], F32, kind="ExternalInput").ap()
    gb_row = nc.dram_tensor("gb_row", [1, E], F32, kind="ExternalInput").ap()
    w1h = nc.dram_tensor("w1h", [n_e, 16, 128, NS * 128], F32, kind="ExternalInput").ap()
    b1h = nc.dram_tensor("b1h", [128, n_e * 16], F32, kind="ExternalInput").ap()
    w2h = nc.dram_tensor("w2h", [n_e, 128, NS * F], F32, kind="ExternalInput").ap()
    b2d = nc.dram_tensor("b2d", [E, F], F32, kind="ExternalInput").ap()
    alibi = nc.dram_tensor("alibi", [128, NH * 8], F32, kind="ExternalInput").ap()
    eye_bf = nc.dram_tensor("eye_bf", [128, 128], BF16, kind="ExternalInput").ap()
    eye_f32 = nc.dram_tensor("eye_f32", [128, 128], F32, kind="ExternalInput").ap()
    s_scr = nc.dram_tensor("s_scr", [n_tok, D], BF16, kind="Internal").ap()
    yout = nc.dram_tensor("yout", [n_tok, D], F32, kind="ExternalOutput").ap()

    es = ExitStack()
    with es:
        tfT = es.enter_context(nc.sbuf_tensor("tfT", [128, NS * n_tok], BF16))
        combT = es.enter_context(nc.sbuf_tensor("combT", [E, n_tok], BF16))
        comb_all = es.enter_context(nc.sbuf_tensor("comb_all", [128, CFG['n_tt'] * E], F32))
        g_all = es.enter_context(nc.sbuf_tensor("g_all", [128, n_tt * E], F32))
        mall = es.enter_context(nc.sbuf_tensor("mall", [128, n_tt], F32))
        rstdall = es.enter_context(nc.sbuf_tensor("rstdall", [128, n_tt], F32))
        bq_sb = es.enter_context(nc.sbuf_tensor("bq_sb", [128, NS], F32))
        bk_sb = es.enter_context(nc.sbuf_tensor("bk_sb", [128, NS], F32))
        bv_sb = es.enter_context(nc.sbuf_tensor("bv_sb", [128, NS], F32))
        gwT_sb = es.enter_context(nc.sbuf_tensor("gwT_sb", [128, NS * E], F32))
        alibi_sb = es.enter_context(nc.sbuf_tensor("alibi_sb", [128, NH * 8], F32))
        eyebf_sb = es.enter_context(nc.sbuf_tensor("eyebf_sb", [128, 128], BF16))
        eyef_sb = es.enter_context(nc.sbuf_tensor("eyef_sb", [128, 128], F32))
        ones_f = es.enter_context(nc.sbuf_tensor("ones_f", [1, 128], F32))
        cbones = es.enter_context(nc.sbuf_tensor("cbones", [1, 128], BF16))
        nsc_bc = es.enter_context(nc.sbuf_tensor("nsc_bc", [128, D], F32))
        gb_bc = es.enter_context(nc.sbuf_tensor("gb_bc", [128, E], F32))
        bo_bc = es.enter_context(nc.sbuf_tensor("bo_bc", [128, D], F32))
        b1_sb = es.enter_context(nc.sbuf_tensor("b1_sb", [128, n_e * 16], F32))
        b2_sb = es.enter_context(nc.sbuf_tensor("b2_sb", [E, F], BF16))
        eps_sb = es.enter_context(nc.sbuf_tensor("eps_sb", [128, 1], F32))
        eps2_sb = es.enter_context(nc.sbuf_tensor("eps2_sb", [128, 1], F32))
        es0 = ExitStack()
        bo_sbr = es0.enter_context(nc.sbuf_tensor("bo_sbr", [1, D], F32))
        nsc_sbr = es0.enter_context(nc.sbuf_tensor("nsc_sbr", [1, D], F32))
        gb_sbr = es0.enter_context(nc.sbuf_tensor("gb_sbr", [1, E], F32))

        s_x = [Sem(es, nc, f"s_x{i}") for i in range(2)]
        s_sl = [Sem(es, nc, f"s_sl{i}") for i in range(2)]
        s_ss = [Sem(es, nc, f"s_ss{i}") for i in range(2)]
        s_y = [Sem(es, nc, f"s_y{i}") for i in range(2)]
        s_c1 = Sem(es, nc, "s_c1")
        s_c2 = Sem(es, nc, "s_c2")
        s_wat = [Sem(es, nc, f"s_wat{i}") for i in range(2)]
        s_w1 = [Sem(es, nc, f"s_w1{i}") for i in range(6)]
        s_w2 = Sem(es, nc, "s_w2")
        s_x6 = [Sem(es, nc, f"s_x6{i}") for i in range(4)]
        s_sl6 = [Sem(es, nc, f"s_sl6{i}") for i in range(4)]
        P_pe = Sem(es, nc, "P_pe")
        P_dve = Sem(es, nc, "P_dve")
        P_act = Sem(es, nc, "P_act")
        s_lc = Sem(es, nc, "s_lc")

        with nc.Block() as blk:
            @blk.sync
            def _(sy):
                for dst, src in ((bq_sb, bq_r), (bk_sb, bk_r), (bv_sb, bv_r),
                                 (bo_sbr, bo_row), (nsc_sbr, nsc_row), (gb_sbr, gb_row),
                                 (alibi_sb, alibi), (eyebf_sb, eye_bf), (eyef_sb, eye_f32),
                                 (gwT_sb, gwT_r), (b1_sb, b1h)):
                    sy.dma_start(out=dst[:], in_=src[:, :]).then_inc(s_c1.h, 16)
                    s_c1.n += 16

            @blk.gpsimd
            def _(gp):
                gp.memset(ones_f[:], 1.0)
                gp.memset(cbones[:], 1.0 / ALPHA)
                gp.memset(eps_sb[:], EPS)
                gp.memset(eps2_sb[:], EPS * EPS)
                gp.dma_start(out=b2_sb[:], in_=b2d[:, :]).then_inc(s_c2.h, 16)
                s_c2.n += 16

            @blk.vector
            def _(v):
                v.wait_ge(s_c1.h, s_c1.n)
                v.wait_ge(s_c2.h, s_c2.n)

        with nc.psum_tensor("pbc2", [128, 1024], F32) as pbc2, \
             nc.psum_tensor("pbc3", [128, 1024], F32) as pbc3, \
             nc.psum_tensor("pbc4", [128, E], F32) as pbc4:
            with nc.Block() as blk:
                @blk.tensor
                def _(te):
                    te.matmul(pbc2[:, 0:512], ones_f[:, :], bo_sbr[:, 0:512], start=True, stop=True)
                    te.matmul(pbc2[:, 512:1024], ones_f[:, :], bo_sbr[:, 512:1024], start=True, stop=True)
                    te.matmul(pbc3[:, 0:512], ones_f[:, :], nsc_sbr[:, 0:512], start=True, stop=True)
                    te.matmul(pbc3[:, 512:1024], ones_f[:, :], nsc_sbr[:, 512:1024], start=True, stop=True)
                    m = te.matmul(pbc4[:, :], ones_f[:, :], gb_sbr[:, :], start=True, stop=True)
                    P_pe.inc(m, 1)

                @blk.vector
                def _(v):
                    v.wait_ge(P_pe.h, P_pe.n)
                    v.tensor_copy(bo_bc[:], pbc2[:, :])
                    v.tensor_copy(nsc_bc[:], pbc3[:, :])
                    c = v.tensor_copy(gb_bc[:], pbc4[:, :])
                    P_dve.inc(c, 1)

                @blk.scalar
                def _(s):
                    s.wait_ge(P_dve.h, P_dve.n)

        es0.close()

        with ExitStack() as sc1:
            xnT_own = sc1.enter_context(nc.sbuf_tensor("xnT_own", [128, NS * n_tok], BF16))
            xnT_win = sc1.enter_context(nc.sbuf_tensor("xnT_win", [128, NS * n_win], BF16))
            sc1a = ExitStack()
            xb = [sc1a.enter_context(nc.sbuf_tensor(f"xb{i}", [128, D], F32)) for i in range(4)]
            st6 = [sc1a.enter_context(nc.sbuf_tensor(f"st6{i}", [128, 12], F32)) for i in range(4)]
            mv = [sc1a.enter_context(nc.sbuf_tensor(f"mv{i}", [128, 2], F32)) for i in range(4)]
            sdb = [sc1a.enter_context(nc.sbuf_tensor(f"sdb{i}", [128, 2], F32)) for i in range(4)]
            rstdb = [sc1a.enter_context(nc.sbuf_tensor(f"rstdb{i}", [128, 2], F32)) for i in range(4)]
            xnbf = [sc1a.enter_context(nc.sbuf_tensor(f"xnbf{i}", [128, D], BF16)) for i in range(4)]
            tf32 = [sc1a.enter_context(nc.sbuf_tensor(f"tf32{i}", [128, D], F32)) for i in range(4)]
            tfTt = [sc1a.enter_context(nc.sbuf_tensor(f"tfTt{i}", [128, D], F32)) for i in range(4)]
            gTsb = sc1a.enter_context(nc.sbuf_tensor("gTsb", [E, 128], F32))

            with nc.psum_tensor("trbf0", [128, 1024], BF16) as trbfA, \
                 nc.psum_tensor("trbf1", [128, 1024], BF16) as trbfB, \
                 nc.psum_tensor("tr320", [128, 1024], F32) as tr32A, \
                 nc.psum_tensor("tr321", [128, 1024], F32) as tr32B, \
                 nc.psum_tensor("gTp", [E, 128], F32) as gTp, \
                 nc.psum_tensor("g2p", [128, E], F32) as g2p:
                trbf = [trbfA, trbfB]
                tr32 = [tr32A, tr32B]

                DVE0, ACT0, PE0 = P_dve.n, P_act.n, P_pe.n
                SX0 = [s_x[0].n, s_x[1].n]

                _d1_pos = {}
                _d1c = [DVE0]

                def _d1reg(lbl):
                    _d1c[0] += 1
                    _d1_pos[lbl] = _d1c[0]
                _d1reg(("A", 0)); _d1reg(("C", 0)); _d1reg(("D", 0))
                for _t in range(n_tt):
                    if _t + 1 < n_tt:
                        _d1reg(("A", _t + 1))
                    _d1reg(("E", _t)); _d1reg(("F", _t))
                    if _t + 1 < n_tt:
                        _d1reg(("C", _t + 1)); _d1reg(("D", _t + 1))
                    _d1reg(("M", _t)); _d1reg(("G", _t))
                assert _d1c[0] == DVE0 + 7 * n_tt

                def dveA(t): return _d1_pos[("A", t)]
                def dveC(t): return _d1_pos[("C", t)]
                def dveD(t): return _d1_pos[("D", t)]
                def dveE(t): return _d1_pos[("E", t)]
                def dveF(t): return _d1_pos[("F", t)]
                def dveGm(t): return _d1_pos[("M", t)]
                def dveG(t): return _d1_pos[("G", t)]
                def actSD(t): return ACT0 + 2 * t + 2
                def peTB(t): return PE0 + 4 * t + 1
                def peT32(t): return PE0 + 4 * t + 2
                def peGm(t): return PE0 + 4 * t + 3
                def peG(t): return PE0 + 4 * t + 4

                with nc.Block() as blk:
                    @blk.sync
                    def _(sy):
                        for t in range(n_tt):
                            pi = t % 4
                            if t >= 4:
                                sy.wait_ge(P_dve.h, dveD(t - 4))
                            sy.dma_start(out=xb[pi][:], in_=xo[t * 128:(t + 1) * 128, :]) \
                              .then_inc(s_x6[pi].h, 16)
                            s_x6[pi].n += 16

                    @blk.vector
                    def _(v):
                        SX60 = [s_x6[i].n - 16 * ((n_tt + 3 - i) // 4) for i in range(4)]

                        def stats(t):
                            pi = t % 4
                            v.wait_ge(s_x6[pi].h, SX60[pi] + 16 * (t // 4 + 1))
                            if t >= 4:
                                v.wait_ge(P_act.h, actSD(t - 4))
                            v.bn_stats(st6[pi][:, 0:6], xb[pi][:, 0:512])
                            v.bn_stats(st6[pi][:, 6:12], xb[pi][:, 512:1024])
                            v.drain()
                            i = v.bn_aggr(mv[pi][:], st6[pi][:])
                            P_dve.inc(i, 1)

                        def norms(t):
                            pi = t % 4
                            v.wait_ge(P_act.h, actSD(t))
                            v.reciprocal(rstdb[pi][:], sdb[pi][:])
                            v.drain()
                            v.tensor_copy(mall[:, t:t + 1], mv[pi][:, 0:1])
                            v.tensor_copy(rstdall[:, t:t + 1], rstdb[pi][:, 0:1])
                            i = v.tensor_scalar(out=xnbf[pi][:], in0=xb[pi][:], scalar1=mv[pi][:, 0:1],
                                                scalar2=rstdb[pi][:, 0:1], op0=ALU.subtract, op1=ALU.mult)
                            P_dve.inc(i, 1)
                            v.drain()
                            v.tensor_scalar(out=tf32[pi][:], in0=xb[pi][:], scalar1=mv[pi][:, 0:1],
                                            scalar2=rstdb[pi][:, 1:2], op0=ALU.subtract, op1=ALU.mult)
                            v.drain()
                            i = v.tensor_tensor(out=tf32[pi][:], in0=tf32[pi][:], in1=nsc_bc[:], op=ALU.mult)
                            P_dve.inc(i, 1)

                        def ef_evacs(t):
                            pi = t % 4
                            v.wait_ge(P_pe.h, peTB(t))
                            dst = bass.AP(xnT_own, t * 128, [[NS * n_tok, 128], [n_tok, NS], [1, 128]])
                            src = bass.AP(trbf[t % 2], 0, [[1024, 128], [128, NS], [1, 128]])
                            i = v.tensor_copy(dst, src)
                            P_dve.inc(i, 1)
                            v.wait_ge(P_pe.h, peT32(t))
                            v.tensor_copy(tfTt[pi][:], tr32[t % 2][:, :])
                            dst2 = bass.AP(tfT, t * 128, [[NS * n_tok, 128], [n_tok, NS], [1, 128]])
                            i = v.tensor_copy(dst2, tr32[t % 2][:, :])
                            P_dve.inc(i, 1)

                        def gm_evacs(t):
                            v.wait_ge(P_pe.h, peGm(t))
                            i = v.tensor_copy(gTsb[:, :], gTp[:, :])
                            P_dve.inc(i, 1)
                            v.wait_ge(P_pe.h, peG(t))
                            i = v.tensor_tensor(out=g_all[:, t * E:(t + 1) * E], in0=g2p[:, :],
                                                in1=gb_bc[:], op=ALU.add)
                            P_dve.inc(i, 1)

                        stats(0)
                        norms(0)
                        for t in range(n_tt):
                            if t + 1 < n_tt:
                                stats(t + 1)
                            ef_evacs(t)
                            if t + 1 < n_tt:
                                norms(t + 1)
                            gm_evacs(t)
                        assert P_dve.n == DVE0 + 7 * n_tt

                    @blk.scalar
                    def _(s):
                        for t in range(n_tt):
                            pi = t % 4
                            s.wait_ge(P_dve.h, dveA(t))
                            P_act.inc(s.activation(out=sdb[pi][:, 0:1], in_=mv[pi][:, 1:2],
                                                   func=AF.Sqrt, bias=eps_sb[:]), 1)
                            P_act.inc(s.activation(out=sdb[pi][:, 1:2], in_=mv[pi][:, 1:2],
                                                   func=AF.Sqrt, scale=1.0 + EPS,
                                                   bias=eps2_sb[:]), 1)
                        assert P_act.n == ACT0 + 2 * n_tt

                    @blk.tensor
                    def _(te):
                        for t in range(n_tt):
                            pi = t % 4
                            te.wait_ge(P_dve.h, dveC(t))
                            if t >= 2:
                                te.wait_ge(P_dve.h, dveE(t - 2))
                            for hs in range(NS):
                                m = te.transpose(trbf[t % 2][:, hs * 128:(hs + 1) * 128],
                                                 xnbf[pi][:, hs * 128:(hs + 1) * 128], eyebf_sb[:])
                            P_pe.inc(m, 1)
                            te.wait_ge(P_dve.h, dveD(t))
                            if t >= 2:
                                te.wait_ge(P_dve.h, dveF(t - 2))
                            for hs in range(NS):
                                m = te.transpose(tr32[t % 2][:, hs * 128:(hs + 1) * 128],
                                                 tf32[pi][:, hs * 128:(hs + 1) * 128], eyef_sb[:])
                            P_pe.inc(m, 1)
                            te.wait_ge(P_dve.h, dveF(t))
                            if t >= 1:
                                te.wait_ge(P_dve.h, dveGm(t - 1))
                            for hs in range(NS):
                                m = te.matmul(gTp[:, :], gwT_sb[:, hs * E:(hs + 1) * E],
                                              tfTt[pi][:, hs * 128:(hs + 1) * 128],
                                              start=(hs == 0), stop=(hs == NS - 1))
                            P_pe.inc(m, 1)
                            te.wait_ge(P_dve.h, dveGm(t))
                            if t >= 1:
                                te.wait_ge(P_dve.h, dveG(t - 1))
                            m = te.transpose(g2p[:, :], gTsb[:, :], eyef_sb[0:32, 0:32])
                            P_pe.inc(m, 1)
                        assert P_pe.n == PE0 + 4 * n_tt

                DVE1, ACT1, PE1 = P_dve.n, P_act.n, P_pe.n
                SX1 = [s_x[0].n, s_x[1].n]

                def dveWA(t): return DVE1 + 3 * t + 1
                def dveWC(t): return DVE1 + 3 * t + 2
                def dveWE(t): return DVE1 + 3 * t + 3
                def actWSD(t): return ACT1 + t + 1
                def peWTB(t): return PE1 + t + 1

                with nc.Block() as blk:
                    @blk.sync
                    def _(sy):
                        for t in range(n_wt):
                            pi = t % 4
                            if t >= 4:
                                sy.wait_ge(P_dve.h, dveWC(t - 4))
                            sy.dma_start(out=xb[pi][:], in_=xw[t * 128:(t + 1) * 128, :]) \
                              .then_inc(s_x6[pi].h, 16)
                            s_x6[pi].n += 16

                    @blk.vector
                    def _(v):
                        SX61 = [s_x6[i].n - 16 * ((n_wt + 3 - i) // 4) for i in range(4)]
                        for t in range(n_wt):
                            pi = t % 4
                            v.wait_ge(s_x6[pi].h, SX61[pi] + 16 * (t // 4 + 1))
                            if t >= 4:
                                v.wait_ge(P_act.h, actWSD(t - 4))
                            v.bn_stats(st6[pi][:, 0:6], xb[pi][:, 0:512])
                            v.bn_stats(st6[pi][:, 6:12], xb[pi][:, 512:1024])
                            v.drain()
                            i = v.bn_aggr(mv[pi][:], st6[pi][:])
                            P_dve.inc(i, 1)
                            v.wait_ge(P_act.h, actWSD(t))
                            v.reciprocal(rstdb[pi][:, 0:1], sdb[pi][:, 0:1])
                            v.drain()
                            i = v.tensor_scalar(out=xnbf[pi][:], in0=xb[pi][:], scalar1=mv[pi][:, 0:1],
                                                scalar2=rstdb[pi][:, 0:1], op0=ALU.subtract, op1=ALU.mult)
                            P_dve.inc(i, 1)
                            v.wait_ge(P_pe.h, peWTB(t))
                            dst = bass.AP(xnT_win, t * 128, [[NS * n_win, 128], [n_win, NS], [1, 128]])
                            src = bass.AP(trbf[t % 2], 0, [[1024, 128], [128, NS], [1, 128]])
                            i = v.tensor_copy(dst, src)
                            P_dve.inc(i, 1)

                    @blk.scalar
                    def _(s):
                        for t in range(n_wt):
                            pi = t % 4
                            s.wait_ge(P_dve.h, dveWA(t))
                            P_act.inc(s.activation(out=sdb[pi][:, 0:1], in_=mv[pi][:, 1:2],
                                                   func=AF.Sqrt, bias=eps_sb[:]), 1)

                    @blk.tensor
                    def _(te):
                        for t in range(n_wt):
                            pi = t % 4
                            te.wait_ge(P_dve.h, dveWC(t))
                            if t >= 2:
                                te.wait_ge(P_dve.h, dveWE(t - 2))
                            for hs in range(NS):
                                m = te.transpose(trbf[t % 2][:, hs * 128:(hs + 1) * 128],
                                                 xnbf[pi][:, hs * 128:(hs + 1) * 128], eyebf_sb[:])
                            P_pe.inc(m, 1)

            m8b = [sc1a.enter_context(nc.sbuf_tensor(f"m8b{i}", [128, 8], F32)) for i in range(2)]
            nmx = [sc1a.enter_context(nc.sbuf_tensor(f"nmx{i}", [128, 1], F32)) for i in range(2)]
            maskb = [sc1a.enter_context(nc.sbuf_tensor(f"maskb{i}", [128, E], F32)) for i in range(2)]
            expb = [sc1a.enter_context(nc.sbuf_tensor(f"expb{i}", [128, E], F32)) for i in range(2)]
            denb = [sc1a.enter_context(nc.sbuf_tensor(f"denb{i}", [128, 1], F32)) for i in range(2)]
            combb = [sc1a.enter_context(nc.sbuf_tensor(f"combb{i}", [128, E], F32)) for i in range(2)]

            DVE2, ACT2, PE2 = P_dve.n, P_act.n, P_pe.n

            def dveH(t): return DVE2 + 3 * t + 1
            def dveJ(t): return DVE2 + 3 * t + 2
            def dveL(t): return DVE2 + 3 * t + 3
            def actI(t): return ACT2 + t + 1
            def peK(t): return PE2 + t + 1

            with nc.psum_tensor("ptr0", [E, 128], F32) as ptr0, \
                 nc.psum_tensor("ptr1", [E, 128], F32) as ptr1:
                ptrs = [ptr0, ptr1]
                with nc.Block() as blk:
                    @blk.vector
                    def _(v):
                        for t in range(n_tt):
                            pi = t % 2
                            gsl = g_all[:, t * E:(t + 1) * E]
                            if t >= 2:
                                v.wait_ge(P_act.h, actI(t - 2))
                            v.max(m8b[pi][:], gsl)
                            v.drain()
                            v.tensor_scalar(out=nmx[pi][:], in0=m8b[pi][:, 0:1], scalar1=-1.0,
                                            scalar2=None, op0=ALU.mult)
                            i = v.tensor_scalar(out=maskb[pi][:], in0=gsl, scalar1=m8b[pi][:, 3:4],
                                                scalar2=None, op0=ALU.is_ge)
                            P_dve.inc(i, 1)
                            v.wait_ge(P_act.h, actI(t))
                            v.tensor_tensor(out=expb[pi][:], in0=expb[pi][:], in1=maskb[pi][:],
                                            op=ALU.mult)
                            v.drain()
                            v.reduce_sum(denb[pi][:], expb[pi][:], axis=AX.X)
                            v.drain()
                            v.reciprocal(denb[pi][:], denb[pi][:])
                            v.drain()
                            i = v.tensor_scalar(out=combb[pi][:], in0=expb[pi][:], scalar1=denb[pi][:],
                                                scalar2=None, op0=ALU.mult)
                            P_dve.inc(i, 1)
                            v.drain()
                            v.tensor_scalar(out=comb_all[:, t * E:(t + 1) * E], in0=combb[pi][:],
                                            scalar1=1.0 / ALPHA, scalar2=None, op0=ALU.mult)
                            v.wait_ge(P_pe.h, peK(t))
                            i = v.tensor_copy(combT[:, t * 128:(t + 1) * 128], ptrs[pi][:, :])
                            P_dve.inc(i, 1)

                    @blk.scalar
                    def _(s):
                        for t in range(n_tt):
                            pi = t % 2
                            s.wait_ge(P_dve.h, dveH(t))
                            P_act.inc(s.activation(out=expb[pi][:], in_=g_all[:, t * E:(t + 1) * E],
                                                   func=AF.Exp, bias=nmx[pi][:]), 1)

                    @blk.tensor
                    def _(te):
                        for t in range(n_tt):
                            pi = t % 2
                            te.wait_ge(P_dve.h, dveJ(t))
                            if t >= 2:
                                te.wait_ge(P_dve.h, dveL(t - 2))
                            P_pe.inc(te.transpose(ptrs[pi][:, :], combb[pi][:, :], eyef_sb[:]), 1)

            sc1a.close()

            wo_sb = sc1.enter_context(nc.sbuf_tensor("wo_sb", [128, NS * D], BF16))
            attnT = sc1.enter_context(nc.sbuf_tensor("attnT", [128, n_pairs * n_tok], BF16))
            wpb = [sc1.enter_context(nc.sbuf_tensor(f"wpb{i}", [128, 3 * 1024], BF16)) for i in range(2)]
            kTb = [sc1.enter_context(nc.sbuf_tensor(f"kTb{i}", [128, n_win], BF16)) for i in range(2)]
            vtb = [sc1.enter_context(nc.sbuf_tensor(f"vtb{i}", [128, n_win], BF16)) for i in range(2)]
            vsb = [sc1.enter_context(nc.sbuf_tensor(f"vsb{i}", [128, n_wt * 2 * 65], BF16)) for i in range(2)]
            qTb = [sc1.enter_context(nc.sbuf_tensor(f"qTb{i}", [128, n_tok], BF16)) for i in range(2)]
            pbuf = [sc1.enter_context(nc.sbuf_tensor(f"pbuf{i}", [128, 512], BF16)) for i in range(4)]
            rlrow = [sc1.enter_context(nc.sbuf_tensor(f"rlrow{i}", [1, 512], F32)) for i in range(2)]
            rlbc = [sc1.enter_context(nc.sbuf_tensor(f"rlbc{i}", [64, 512], F32)) for i in range(2)]
            sstage = [sc1.enter_context(nc.sbuf_tensor(f"sstage{i}", [128, D], BF16)) for i in range(2)]

            with nc.psum_tensor("kvps", [128, 1024], F32) as kvps, \
                 nc.psum_tensor("trb2", [128, 1024], BF16) as trb2, \
                 nc.psum_tensor("scpP", [128, 1024], F32) as scpP, \
                 nc.psum_tensor("avpA", [128, 512], F32) as avpA, \
                 nc.psum_tensor("avpB", [128, 512], F32) as avpB, \
                 nc.psum_tensor("rlp", [64, 512], F32) as rlp:
                scp = [scpP[:, 0:512], scpP[:, 512:1024], kvps[:, 0:512], kvps[:, 512:1024]]
                avp = [avpA, avpB]

                PE4, ACT4, DVE4 = P_pe.n, P_act.n, P_dve.n
                n_u = 2 * n_qc
                per_pair_act = 4 + n_u * n_wt
                per_pair_dve = 1 + n_u

                _pe_pos = {}
                _ctr = [PE4]

                def _reg(label):
                    _ctr[0] += 1
                    _pe_pos[label] = _ctr[0]
                for _hp in range(n_pairs):
                    if _hp > 0:
                        _reg(("rlb", _hp - 1, n_u - 1))
                    _reg(("kt", _hp))
                    _reg(("vt", _hp))
                    _reg(("vtr", _hp))
                    _reg(("q", _hp, 0))
                    _reg(("q", _hp, 1))
                    for _u in range(n_u):
                        for _kt in range(4):
                            _reg(("sc", _hp, _u, _kt))
                        if _u > 0:
                            _reg(("rlb", _hp, _u - 1))
                        for _kt in range(n_wt):
                            _reg(("av", _hp, _u, _kt))
                            if _kt + 4 < n_wt:
                                _reg(("sc", _hp, _u, _kt + 4))
                _reg(("rlb", n_pairs - 1, n_u - 1))
                PE4_END = _ctr[0]

                def pe4_kt(hp): return _pe_pos[("kt", hp)]
                def pe4_vt(hp): return _pe_pos[("vt", hp)]
                def pe4_vtr(hp): return _pe_pos[("vtr", hp)]
                def pe4_q(hp, qch): return _pe_pos[("q", hp, qch)]
                def pe4_sc(hp, u, kt): return _pe_pos[("sc", hp, u, kt)]
                def pe4_av(hp, u, kt): return _pe_pos[("av", hp, u, kt)]
                def pe4_rlb(hp, u): return _pe_pos[("rlb", hp, u)]

                def act4_ktev(hp): return ACT4 + hp * per_pair_act + 1
                def act4_vtev(hp): return ACT4 + hp * per_pair_act + 2
                def act4_qev(hp, qch): return ACT4 + hp * per_pair_act + 3 + qch
                def act4_exp(hp, u, kt): return ACT4 + hp * per_pair_act + 4 + u * n_wt + kt + 1

                _dve_pos = {}
                _dctr = [DVE4]

                def _dreg(label):
                    _dctr[0] += 1
                    _dve_pos[label] = _dctr[0]
                for _hp in range(n_pairs):
                    _dreg(("vev", _hp))
                    for _u in range(n_u):
                        _dreg(("unit", _hp, _u))
                DVE4_END = _dctr[0]

                def dve4_vev(hp): return _dve_pos[("vev", hp)]
                def dve4_unit(hp, u): return _dve_pos[("unit", hp, u)]

                with nc.Block() as blk:
                    @blk.gpsimd
                    def _(gp):
                        for pi in range(2):
                            ap = bass.AP(vsb[pi], 64, [[n_wt * 2 * 65, 128], [65, n_wt * 2], [1, 1]])
                            gp.memset(ap, 1.0)
                        for hp in range(n_pairs):
                            pi = hp % 2
                            if hp >= 2:
                                gp.wait_ge(P_pe.h, pe4_av(hp - 2, n_u - 1, n_wt - 1))
                            for wsrc, off in ((wq, 0), (wk, 1024), (wv, 2048)):
                                src = bass.AP(wsrc.tensor, hp * 128,
                                              [[1024, 128], [131072, NS], [1, 128]])
                                dst = bass.AP(wpb[pi], off, [[3 * 1024, 128], [128, NS], [1, 128]])
                                gp.dma_start(out=dst, in_=src).then_inc(s_wat[pi].h, 16)
                                s_wat[pi].n += 16
                        srcwo = bass.AP(wo.tensor, 0, [[1024, 128], [131072, NS], [1, 1024]])
                        gp.dma_start(out=wo_sb[:], in_=srcwo).then_inc(s_c2.h, 16)
                        s_c2.n += 16

                    @blk.tensor
                    def _(te):
                        wat = [0, 0]

                        def _emit_rlb(rhp, ru):
                            rupi = ru % 2
                            te.wait_ge(s_lc.h, rhp * n_u + ru + 1)
                            if ru >= 1:
                                te.wait_ge(P_dve.h, dve4_unit(rhp, ru - 1))
                            elif rhp >= 1:
                                te.wait_ge(P_dve.h, dve4_unit(rhp - 1, n_u - 1))
                            m = te.matmul(rlp[:, :], ones_f[:, 0:64], rlrow[rupi][:, :],
                                          start=True, stop=True)
                            P_pe.inc(m, 1)
                        emit_rlb = [_emit_rlb]

                        for hp in range(n_pairs):
                            pi = hp % 2
                            wat[pi] += 48
                            te.wait_ge(s_wat[pi].h, wat[pi])
                            if hp > 0:
                                emit_rlb[0](hp - 1, n_u - 1)
                            if hp >= 1:
                                te.wait_ge(P_act.h, act4_qev(hp - 1, 1))
                            for kc in range(n_win // 512):
                                for hs in range(NS):
                                    m = te.matmul(kvps[:, kc * 512:(kc + 1) * 512],
                                                  wpb[pi][:, 1024 + hs * 128:1024 + (hs + 1) * 128],
                                                  xnT_win[:, hs * n_win + kc * 512:hs * n_win + (kc + 1) * 512],
                                                  start=(hs == 0), stop=(hs == NS - 1))
                            P_pe.inc(m, 1)
                            for kc in range(n_win // 512):
                                for hs in range(NS):
                                    m = te.matmul(scpP[:, kc * 512:(kc + 1) * 512],
                                                  wpb[pi][:, 2048 + hs * 128:2048 + (hs + 1) * 128],
                                                  xnT_win[:, hs * n_win + kc * 512:hs * n_win + (kc + 1) * 512],
                                                  start=(hs == 0), stop=(hs == NS - 1))
                            P_pe.inc(m, 1)
                            te.wait_ge(P_act.h, act4_ktev(hp))
                            te.wait_ge(P_act.h, act4_vtev(hp))
                            if hp >= 1:
                                te.wait_ge(P_dve.h, dve4_vev(hp - 1))
                            for kt in range(n_wt):
                                m = te.transpose(trb2[:, kt * 128:(kt + 1) * 128],
                                                 vtb[pi][:, kt * 128:(kt + 1) * 128], eyebf_sb[:])
                            P_pe.inc(m, 1)
                            for qch in range(2):
                                if qch == 1:
                                    te.wait_ge(P_act.h, act4_qev(hp, 0))
                                for qc in range(n_qc // 2):
                                    col = qch * (n_qc // 2) + qc
                                    for hs in range(NS):
                                        m = te.matmul(kvps[:, qc * 512:(qc + 1) * 512],
                                                      wpb[pi][:, hs * 128:(hs + 1) * 128],
                                                      xnT_own[:, hs * n_tok + col * 512:hs * n_tok + (col + 1) * 512],
                                                      start=(hs == 0), stop=(hs == NS - 1))
                                P_pe.inc(m, 1)
                            te.wait_ge(P_act.h, act4_qev(hp, 1))
                            te.wait_ge(P_dve.h, dve4_vev(hp))
                            for u in range(n_u):
                                hsub = u // n_qc
                                qc = u % n_qc
                                p0 = hsub * 64
                                upi = u % 2
                                if u >= 2:
                                    te.wait_ge(P_dve.h, dve4_unit(hp, u - 2))
                                elif hp >= 1:
                                    te.wait_ge(P_dve.h, dve4_unit(hp - 1, n_u - 2 + u))
                                def emit_sc(kt):
                                    kpi = kt % 4
                                    if kt >= 4:
                                        te.wait_ge(P_act.h, act4_exp(hp, u, kt - 4))
                                    elif u >= 1:
                                        te.wait_ge(P_act.h, act4_exp(hp, u - 1, n_wt - 4 + kt))
                                    elif hp >= 1:
                                        te.wait_ge(P_act.h, act4_exp(hp - 1, n_u - 1, n_wt - 4 + kt))
                                    m = te.matmul(scp[kpi][:, :],
                                                  kTb[pi][p0:p0 + 64, kt * 128:(kt + 1) * 128],
                                                  qTb[pi][p0:p0 + 64, qc * 512:(qc + 1) * 512],
                                                  start=True, stop=True)
                                    P_pe.inc(m, 1)

                                def emit_av(kt):
                                    kpi = kt % 4
                                    te.wait_ge(P_act.h, act4_exp(hp, u, kt))
                                    m = te.matmul(avp[upi][0:65, :],
                                                  vsb[pi][:, (kt * 2 + hsub) * 65:(kt * 2 + hsub) * 65 + 65],
                                                  pbuf[kpi][:, :],
                                                  start=(kt == 0), stop=(kt == n_wt - 1))
                                    P_pe.inc(m, 1)

                                for kt in range(4):
                                    emit_sc(kt)
                                if u > 0:
                                    emit_rlb[0](hp, u - 1)
                                for kt in range(n_wt):
                                    emit_av(kt)
                                    if kt + 4 < n_wt:
                                        emit_sc(kt + 4)
                        emit_rlb[0](n_pairs - 1, n_u - 1)
                        assert P_pe.n == PE4_END

                    @blk.scalar
                    def _(s):
                        for hp in range(n_pairs):
                            pi = hp % 2
                            s.wait_ge(P_pe.h, pe4_kt(hp))
                            P_act.inc(s.activation(out=kTb[pi][:], in_=kvps[:, 0:n_win],
                                                   func=AF.Identity, bias=bk_sb[:, hp:hp + 1]), 1)
                            s.wait_ge(P_pe.h, pe4_vt(hp))
                            P_act.inc(s.activation(out=vtb[pi][:], in_=scpP[:, 0:n_win],
                                                   func=AF.Identity, bias=bv_sb[:, hp:hp + 1]), 1)
                            for qch in range(2):
                                s.wait_ge(P_pe.h, pe4_q(hp, qch))
                                P_act.inc(s.activation(
                                    out=qTb[pi][:, qch * (n_tok // 2):(qch + 1) * (n_tok // 2)],
                                    in_=kvps[:, 0:n_tok // 2], func=AF.Identity,
                                    bias=bq_sb[:, hp:hp + 1]), 1)
                            for u in range(n_u):
                                hsub = u // n_qc
                                h = 2 * hp + hsub
                                for kt in range(n_wt):
                                    kpi = kt % 4
                                    s.wait_ge(P_pe.h, pe4_sc(hp, u, kt))
                                    if kt >= 4:
                                        s.wait_ge(P_pe.h, pe4_av(hp, u, kt - 4))
                                    elif u >= 1:
                                        s.wait_ge(P_pe.h, pe4_av(hp, u - 1, n_wt - 4 + kt))
                                    elif hp >= 1:
                                        s.wait_ge(P_pe.h, pe4_av(hp - 1, n_u - 1, n_wt - 4 + kt))
                                    P_act.inc(s.activation(out=pbuf[kpi][:, :], in_=scp[kpi][:, :],
                                                           func=AF.Exp,
                                                           bias=alibi_sb[:, h * 8 + kt:h * 8 + kt + 1],
                                                           scale=0.125), 1)
                                s.wait_ge(P_pe.h, pe4_av(hp, u, n_wt - 1))
                                s_lc.inc(s.activation(out=rlrow[u % 2][:, :],
                                                      in_=avp[u % 2][64:65, :],
                                                      func=AF.Copy), 1)
                        assert P_act.n == ACT4 + n_pairs * per_pair_act

                    @blk.vector
                    def _(v):
                        def finish_unit(fhp, fu):
                            fhsub = fu // n_qc
                            fqc = fu % n_qc
                            fupi = fu % 2
                            v.wait_ge(P_pe.h, pe4_rlb(fhp, fu))
                            v.reciprocal(rlbc[fupi][:, :], rlp[:, :])
                            v.drain()
                            i = v.tensor_tensor(
                                out=attnT[fhsub * 64:fhsub * 64 + 64,
                                          fhp * n_tok + fqc * 512:fhp * n_tok + (fqc + 1) * 512],
                                in0=avp[fupi][0:64, :], in1=rlbc[fupi][:, :], op=ALU.mult)
                            P_dve.inc(i, 1)

                        for hp in range(n_pairs):
                            pi = hp % 2
                            v.wait_ge(P_pe.h, pe4_vtr(hp))
                            for hsub in range(2):
                                dst = bass.AP(vsb[pi], hsub * 65,
                                              [[n_wt * 2 * 65, 128], [130, n_wt], [1, 64]])
                                src = bass.AP(trb2, hsub * 64, [[1024, 128], [128, n_wt], [1, 64]])
                                i = v.tensor_copy(dst, src)
                            P_dve.inc(i, 1)
                            for u in range(n_u):
                                finish_unit(hp, u)
                        assert P_dve.n == DVE4_END

                DVE4b, PE4b = P_dve.n, P_pe.n

                def pe_wo(t): return PE4b + 2 * t + 2
                def dve_wo(t): return DVE4b + t + 1

                with nc.Block() as blk:
                    @blk.tensor
                    def _(te):
                        te.wait_ge(s_c2.h, s_c2.n)
                        for t in range(n_tt):
                            pi = t % 2
                            if t >= 2:
                                te.wait_ge(P_dve.h, dve_wo(t - 2))
                            for dmh in range(2):
                                ps = scpP[:, pi * 512:(pi + 1) * 512] if dmh == 0 else avp[pi][:, :]
                                for sl in range(n_pairs):
                                    m = te.matmul(ps,
                                                  attnT[:, sl * n_tok + t * 128:sl * n_tok + (t + 1) * 128],
                                                  wo_sb[:, sl * D + dmh * 512:sl * D + (dmh + 1) * 512],
                                                  start=(sl == 0), stop=(sl == n_pairs - 1))
                                P_pe.inc(m, 1)

                    @blk.vector
                    def _(v):
                        for t in range(n_tt):
                            pi = t % 2
                            v.wait_ge(P_pe.h, pe_wo(t))
                            if t >= 2:
                                v.wait_ge(s_ss[pi].h, 16 * ((t - 2) // 2 + 1))
                            v.tensor_tensor(out=sstage[pi][:, 0:512],
                                            in0=scpP[:, pi * 512:(pi + 1) * 512],
                                            in1=bo_bc[:, 0:512], op=ALU.add)
                            i = v.tensor_tensor(out=sstage[pi][:, 512:1024], in0=avp[pi][:, :],
                                                in1=bo_bc[:, 512:1024], op=ALU.add)
                            P_dve.inc(i, 1)

                    @blk.sync
                    def _(sy):
                        for t in range(n_tt):
                            pi = t % 2
                            sy.wait_ge(P_dve.h, dve_wo(t))
                            sy.dma_start(out=s_scr[t * 128:(t + 1) * 128, :], in_=sstage[pi][:]) \
                              .then_inc(s_ss[pi].h, 16)
                            s_ss[pi].n += 16

        with ExitStack() as sc2:
            out2 = sc2.enter_context(nc.sbuf_tensor("out2", [128, n_tt * D], F32))
            sc2a = ExitStack()
            hwT = sc2a.enter_context(nc.sbuf_tensor("hwT", [128, NS * n_tok], BF16))
            w1r = [sc2a.enter_context(nc.sbuf_tensor(f"w1r{i}", [128, NS * 128], BF16)) for i in range(6)]
            w2_sb = sc2a.enter_context(nc.sbuf_tensor("w2_sb", [128, NS * F], BF16))
            t1b = [sc2a.enter_context(nc.sbuf_tensor(f"t1b{i}", [128, 512], F32)) for i in range(2)]
            t2b = [sc2a.enter_context(nc.sbuf_tensor(f"t2b{i}", [128, 512], F32)) for i in range(2)]
            ub = [sc2a.enter_context(nc.sbuf_tensor(f"ub{i}", [128, 512], F32)) for i in range(2)]
            hps = [sc2a.enter_context(nc.psum_tensor(f"hps{i}", [128, 512], F32)) for i in range(8)]

            PEc = P_pe.n
            with nc.Block() as blk:
                @blk.tensor
                def _(te):
                    k = 0
                    for t in range(n_tt):
                        for dmh in range(2):
                            k += 1
                            if k > 4:
                                te.wait_ge(P_dve.h, P_dve.n + k - 4)
                            P_pe.inc(te.matmul(hps[(k - 1) % 4][:, :],
                                               combT[:, t * 128:(t + 1) * 128],
                                               b2_sb[:, dmh * 512:(dmh + 1) * 512],
                                               start=True, stop=True), 1)

                @blk.vector
                def _(v):
                    k = 0
                    for t in range(n_tt):
                        for dmh in range(2):
                            k += 1
                            v.wait_ge(P_pe.h, PEc + k)
                            c = v.tensor_copy(out2[:, t * D + dmh * 512:t * D + (dmh + 1) * 512],
                                              hps[(k - 1) % 4][:, :])
                            P_dve.inc(c, 1)

            PE5, DVE5, ACT5 = P_pe.n, P_dve.n, P_act.n
            sw_units = NS * n_half * n_tc
            h1_incs = NS * n_half * 2 * n_tc
            w2_incs = n_tt * 2
            pe_per_e = h1_incs + w2_incs
            dve_per_e = 2 * sw_units + w2_incs
            act_per_e = sw_units

            def pe5_h1(e, j, half, gl, tc):
                return PE5 + e * pe_per_e + ((j * n_half + half) * 2 + gl) * n_tc + tc + 1

            def pe5_w2(e, t, dmh):
                return PE5 + e * pe_per_e + h1_incs + t * 2 + dmh + 1

            def dve5_mid(e, j, half, tc):
                return DVE5 + e * dve_per_e + ((j * n_half + half) * n_tc + tc) * 2 + 1

            def dve5_sw(e, j, half, tc):
                return DVE5 + e * dve_per_e + ((j * n_half + half) * n_tc + tc) * 2 + 2

            def dve5_w2(e, t, dmh):
                return DVE5 + e * dve_per_e + 2 * sw_units + t * 2 + dmh + 1

            def act5_silu(e, j, half, tc):
                return ACT5 + e * act_per_e + (j * n_half + half) * n_tc + tc + 1

            def w1_slot(e, j, gl):
                return (e * 16 + 2 * j + gl) % 6

            def w1_consumed(e, j, gl):
                return pe5_h1(e, j, 1, gl, n_tc - 1)

            with nc.Block() as blk:
                @blk.gpsimd
                def _(gp):
                    for e in range(n_e):
                        for j in range(NS):
                            for gl in range(2):
                                idx = e * 16 + 2 * j + gl
                                if idx >= 6:
                                    pidx = idx - 6
                                    pe_, rem = divmod(pidx, 16)
                                    pj, pgl = rem // 2, rem % 2
                                    gp.wait_ge(P_pe.h, w1_consumed(pe_, pj, pgl))
                                slot = w1_slot(e, j, gl)
                                blkid = gl * 8 + j
                                gp.dma_start(out=w1r[slot][:], in_=w1h[e, blkid, :, :]) \
                                  .then_inc(s_w1[slot].h, 16)
                                s_w1[slot].n += 16
                        if e == 0:
                            gp.dma_start(out=w2_sb[:], in_=w2h[0, :, :]).then_inc(s_w2.h, 16)
                        else:
                            gp.wait_ge(P_pe.h, pe5_w2(e - 1, n_tt - 1, 1))
                            gp.dma_start(out=w2_sb[:], in_=w2h[e, :, :]).then_inc(s_w2.h, 16)
                        s_w2.n += 16

                @blk.tensor
                def _(te):
                    w1cnt = [0] * 6
                    for e in range(n_e):
                        for j in range(NS):
                            for gl in range(2):
                                w1cnt[w1_slot(e, j, gl)] += 16
                            for half in range(n_half):
                                gidx = j * n_half + half
                                grp = gidx % 2
                                if gidx >= 2:
                                    pg = gidx - 2
                                    te.wait_ge(P_dve.h, dve5_sw(e, pg // n_half, pg % n_half, n_tc - 1))
                                elif e >= 1:
                                    if grp == 0:
                                        te.wait_ge(P_dve.h, dve5_w2(e - 1, n_tt - 1, 1))
                                    else:
                                        te.wait_ge(P_dve.h, dve5_sw(e - 1, NS - 1, n_half - 1, n_tc - 1))
                                for gl in range(2):
                                    slot = w1_slot(e, j, gl)
                                    te.wait_ge(s_w1[slot].h, w1cnt[slot])
                                    for hs in range(NS):
                                        for tc in range(n_tc):
                                            bank = grp * 4 + gl * n_tc + tc
                                            tcol = half * (n_tc * 512) + tc * 512
                                            m = te.matmul(
                                                hps[bank][:, :],
                                                w1r[slot][:, hs * 128:(hs + 1) * 128],
                                                tfT[:, hs * n_tok + tcol:hs * n_tok + tcol + 512],
                                                start=(hs == 0), stop=(hs == NS - 1))
                                            if hs == NS - 1:
                                                P_pe.inc(m, 1)
                        te.wait_ge(s_w2.h, 16 * (e + 1))
                        for t in range(n_tt):
                            if t == 0:
                                te.wait_ge(P_dve.h, dve5_sw(e, NS - 1, 0, n_tc - 1))
                            elif t == n_tt // 2:
                                te.wait_ge(P_dve.h, dve5_sw(e, NS - 1, n_half - 1, n_tc - 1))
                            for dmh in range(2):
                                if t >= 2:
                                    te.wait_ge(P_dve.h, dve5_w2(e, t - 2, dmh))
                                elif e >= 1:
                                    te.wait_ge(P_dve.h, dve5_w2(e - 1, n_tt - 2 + t, dmh))
                            mm_last = [None, None]
                            for fs in range(NS):
                                for dmh in range(2):
                                    bank = (t % 2) * 2 + dmh
                                    mm_last[dmh] = te.matmul(
                                        hps[bank][:, :],
                                        hwT[:, fs * n_tok + t * 128:fs * n_tok + (t + 1) * 128],
                                        w2_sb[:, fs * F + dmh * 512:fs * F + (dmh + 1) * 512],
                                        start=(fs == 0), stop=(fs == NS - 1))
                            P_pe.inc(mm_last[0], 1)
                            P_pe.inc(mm_last[1], 1)
                    assert P_pe.n == PE5 + n_e * pe_per_e

                @blk.vector
                def _(v):
                    for e in range(n_e):
                        for j in range(NS):
                            for half in range(n_half):
                                gidx = j * n_half + half
                                grp = gidx % 2
                                for tc in range(n_tc):
                                    gbank = grp * 4 + 0 * n_tc + tc
                                    lbank = grp * 4 + 1 * n_tc + tc
                                    tcol = half * (n_tc * 512) + tc * 512
                                    pi = (gidx * n_tc + tc) % 2
                                    v.wait_ge(P_pe.h, pe5_h1(e, j, half, 0, tc))
                                    i = v.tensor_scalar(out=t1b[pi][:], in0=hps[gbank][:, :],
                                                        scalar1=b1_sb[:, e * 16 + j:e * 16 + j + 1],
                                                        scalar2=LIMIT, op0=ALU.add, op1=ALU.min)
                                    P_dve.inc(i, 1)
                                    v.wait_ge(P_pe.h, pe5_h1(e, j, half, 1, tc))
                                    v.tensor_scalar(out=t2b[pi][:], in0=hps[lbank][:, :],
                                                    scalar1=b1_sb[:, e * 16 + 8 + j:e * 16 + 8 + j + 1],
                                                    scalar2=LIMIT, op0=ALU.add, op1=ALU.min)
                                    v.drain()
                                    v.tensor_scalar(out=t2b[pi][:], in0=t2b[pi][:], scalar1=-LIMIT,
                                                    scalar2=1.0, op0=ALU.max, op1=ALU.add)
                                    v.wait_ge(P_act.h, act5_silu(e, j, half, tc))
                                    v.drain()
                                    i = v.tensor_tensor(
                                        out=hwT[:, j * n_tok + tcol:j * n_tok + tcol + 512],
                                        in0=ub[pi][:], in1=t2b[pi][:], op=ALU.mult)
                                    P_dve.inc(i, 1)
                        for t in range(n_tt):
                            for dmh in range(2):
                                bank = (t % 2) * 2 + dmh
                                v.wait_ge(P_pe.h, pe5_w2(e, t, dmh))
                                dst = out2[:, t * D + dmh * 512:t * D + (dmh + 1) * 512]
                                i = v.scalar_tensor_tensor(out=dst, in0=hps[bank][:, :],
                                                           scalar=comb_all[:, t * E + e:t * E + e + 1],
                                                           in1=dst, op0=ALU.mult, op1=ALU.add)
                                P_dve.inc(i, 1)
                        v.drain()
                    assert P_dve.n == DVE5 + n_e * dve_per_e

                @blk.scalar
                def _(s):
                    ucount = 0
                    units = []
                    for e in range(n_e):
                        for j in range(NS):
                            for half in range(n_half):
                                for tc in range(n_tc):
                                    gidx = j * n_half + half
                                    pi = (gidx * n_tc + tc) % 2
                                    units.append((e, j, half, tc))
                                    s.wait_ge(P_dve.h, dve5_mid(e, j, half, tc))
                                    if ucount >= 2:
                                        pe_, pj, ph, ptc = units[ucount - 2]
                                        s.wait_ge(P_dve.h, dve5_sw(pe_, pj, ph, ptc))
                                    P_act.inc(s.activation(out=ub[pi][:], in_=t1b[pi][:],
                                                           func=AF.Silu, scale=ALPHA), 1)
                                    ucount += 1

            sc2a.close()

            xb2 = [sc2.enter_context(nc.sbuf_tensor(f"xb2{i}", [128, D], F32)) for i in range(4)]
            slb = [sc2.enter_context(nc.sbuf_tensor(f"slb{i}", [128, D], BF16)) for i in range(4)]
            xn32 = [sc2.enter_context(nc.sbuf_tensor(f"xn32{i}", [128, D], F32)) for i in range(4)]
            f1b = [sc2.enter_context(nc.sbuf_tensor(f"f1b{i}", [128, D], F32)) for i in range(4)]
            gb2 = [sc2.enter_context(nc.sbuf_tensor(f"gb2{i}", [128, D], F32)) for i in range(4)]
            ytb = [sc2.enter_context(nc.sbuf_tensor(f"ytb{i}", [128, D], F32)) for i in range(4)]

            DVE6, ACT6 = P_dve.n, P_act.n
            SX6B = [s_x6[i].n for i in range(4)]
            SL6B = [s_sl6[i].n for i in range(4)]

            _d6_pos = {}
            _d6c = [DVE6]

            def _d6reg(lbl):
                _d6c[0] += 1
                _d6_pos[lbl] = _d6c[0]
            _d6reg(("f", 0))
            for _t in range(n_tt):
                if _t + 1 < n_tt:
                    _d6reg(("f", _t + 1))
                _d6reg(("y", _t))
            assert _d6c[0] == DVE6 + 2 * n_tt

            def dve6_f1(t): return _d6_pos[("f", t)]
            def dve6_y(t): return _d6_pos[("y", t)]
            def act6_g(t): return ACT6 + t + 1

            with nc.Block() as blk:
                @blk.sync
                def _(sy):
                    sy.wait_ge(s_ss[0].h, s_ss[0].n)
                    sy.wait_ge(s_ss[1].h, s_ss[1].n)

                    def load(t):
                        sl4 = t % 4
                        sy.dma_start(out=xb2[sl4][:], in_=xo[t * 128:(t + 1) * 128, :]) \
                          .then_inc(s_x6[sl4].h, 16)
                        s_x6[sl4].n += 16
                        sy.dma_start(out=slb[sl4][:], in_=s_scr[t * 128:(t + 1) * 128, :]) \
                          .then_inc(s_sl6[sl4].h, 16)
                        s_sl6[sl4].n += 16

                    for t0 in range(min(4, n_tt)):
                        load(t0)
                    for t in range(n_tt):
                        pi = t % 4
                        sy.wait_ge(P_dve.h, dve6_y(t))
                        sy.dma_start(out=yout[t * 128:(t + 1) * 128, :], in_=ytb[pi][:]) \
                          .then_inc(s_y[t % 2].h, 16)
                        s_y[t % 2].n += 16
                        if t + 4 < n_tt:
                            load(t + 4)
                    sy.wait_ge(s_y[0].h, s_y[0].n)
                    sy.wait_ge(s_y[1].h, s_y[1].n)

                @blk.vector
                def _(v):
                    def fpart(t):
                        pi = t % 4
                        v.wait_ge(s_x6[pi].h, SX6B[pi] + 16 * (t // 4 + 1))
                        v.tensor_scalar(out=xn32[pi][:], in0=xb2[pi][:], scalar1=mall[:, t:t + 1],
                                        scalar2=rstdall[:, t:t + 1], op0=ALU.subtract, op1=ALU.mult)
                        v.drain()
                        i = v.tensor_tensor(out=f1b[pi][:], in0=xn32[pi][:],
                                            in1=out2[:, t * D:(t + 1) * D], op=ALU.add)
                        P_dve.inc(i, 1)

                    def ypart(t):
                        pi = t % 4
                        si = t % 2
                        v.wait_ge(P_act.h, act6_g(t))
                        v.wait_ge(s_sl6[pi].h, SL6B[pi] + 16 * (t // 4 + 1))
                        if t >= 2:
                            v.wait_ge(s_y[si].h, 16 * ((t - 2) // 2 + 1))
                        v.tensor_tensor(out=ytb[pi][:], in0=gb2[pi][:], in1=slb[pi][:], op=ALU.mult)
                        v.drain()
                        i = v.tensor_tensor(out=ytb[pi][:], in0=ytb[pi][:], in1=xb2[pi][:], op=ALU.add)
                        P_dve.inc(i, 1)

                    fpart(0)
                    for t in range(n_tt):
                        if t + 1 < n_tt:
                            fpart(t + 1)
                        ypart(t)

                @blk.scalar
                def _(s):
                    for t in range(n_tt):
                        pi = t % 4
                        s.wait_ge(P_dve.h, dve6_f1(t))
                        if t >= 4:
                            s.wait_ge(P_dve.h, dve6_y(t - 4))
                        P_act.inc(s.activation(out=gb2[pi][:], in_=f1b[pi][:], func=AF.Silu), 1)

    return nc


def prep_inputs(inputs, n_e=32):
    x = np.ascontiguousarray(inputs["L0_attn_x"], dtype=np.float32)
    out = {}
    out["wq"] = np.ascontiguousarray(inputs["L0_attn_Wq"], dtype=np.float32)
    out["wk"] = np.ascontiguousarray(inputs["L0_attn_Wk"], dtype=np.float32)
    out["wv"] = np.ascontiguousarray(inputs["L0_attn_Wv"], dtype=np.float32)
    out["wo"] = np.ascontiguousarray(inputs["L0_attn_Wo"], dtype=np.float32)
    out["bq_r"] = np.ascontiguousarray(np.asarray(inputs["L0_attn_bq"], dtype=np.float32).reshape(NS, 128).T)
    out["bk_r"] = np.ascontiguousarray(np.asarray(inputs["L0_attn_bk"], dtype=np.float32).reshape(NS, 128).T)
    out["bv_r"] = np.ascontiguousarray(np.asarray(inputs["L0_attn_bv"], dtype=np.float32).reshape(NS, 128).T)
    out["bo_row"] = np.ascontiguousarray(np.asarray(inputs["L0_attn_bo"], dtype=np.float32).reshape(1, D))
    out["nsc_row"] = np.ascontiguousarray(np.asarray(inputs["L0_ffn_norm_scale"], dtype=np.float32).reshape(1, D))
    gw = np.asarray(inputs["L0_ffn_gate_w"], dtype=np.float32)
    out["gwT_r"] = np.ascontiguousarray(gw.T.reshape(NS, 128, E).transpose(1, 0, 2).reshape(128, NS * E))
    out["gb_row"] = np.ascontiguousarray(np.asarray(inputs["L0_ffn_gate_b"], dtype=np.float32).reshape(1, E))
    w1 = np.asarray(inputs["L0_ffn_w1"], dtype=np.float32)
    w1r = np.concatenate([w1[:, 0::2, :], w1[:, 1::2, :]], axis=1)
    w1t = w1r.transpose(0, 2, 1)
    w1h = w1t.reshape(E, NS, 128, 16, 128).transpose(0, 3, 2, 1, 4)
    out["w1h"] = np.ascontiguousarray(w1h.reshape(E, 16, 128, NS * 128)[:n_e])
    b1 = np.asarray(inputs["L0_ffn_b1"], dtype=np.float32)
    b1r = np.concatenate([b1[:, 0::2], b1[:, 1::2]], axis=1)
    out["b1h"] = np.ascontiguousarray(b1r.reshape(E, 16, 128).transpose(2, 0, 1).reshape(128, E * 16)[:, :n_e * 16])
    w2 = np.asarray(inputs["L0_ffn_w2"], dtype=np.float32)
    w2t = w2.transpose(0, 2, 1)
    out["w2h"] = np.ascontiguousarray(w2t.reshape(E, NS, 128, D).transpose(0, 2, 1, 3).reshape(E, 128, NS * D)[:n_e])
    out["b2d"] = np.ascontiguousarray(np.asarray(inputs["L0_ffn_b2"], dtype=np.float32))
    slopes = 1.0 / 2.0 ** (np.arange(NH, dtype=np.float64) / NH)
    iot = np.arange(128, dtype=np.float64)
    al = np.zeros((128, NH * 8), dtype=np.float32)
    for h in range(NH):
        for kt in range(8):
            kg = (SEQ - WIN) + kt * 128 + iot
            al[:, h * 8 + kt] = (slopes[h] * (kg - (SEQ - 1)) - BOFF).astype(np.float32)
    out["alibi"] = al
    out["eye_bf"] = np.eye(128, dtype=ml_dtypes.bfloat16)
    out["eye_f32"] = np.eye(128, dtype=np.float32)
    return x, out


def kernel(**inputs):
    x, shared = prep_inputs(inputs)
    in_maps = []
    for c in range(8):
        b = c // 4
        q0 = (c % 4) * TPC
        m = dict(shared)
        m["xo"] = np.ascontiguousarray(x[b, q0:q0 + TPC, :])
        m["xw"] = np.ascontiguousarray(x[b, SEQ - WIN:, :])
        in_maps.append(m)
    nc = build_program()
    res = run_bass_kernel_spmd(nc, in_maps, core_ids=list(range(8)))
    y = np.empty((2, SEQ, D), dtype=np.float32)
    for c in range(8):
        b = c // 4
        q0 = (c % 4) * TPC
        y[b, q0:q0 + TPC, :] = np.asarray(res.results[c]["yout"]).reshape(TPC, D)
    return y
